# version 18
# baseline (speedup 1.0000x reference)
import sys
if '/opt/trn_rl_repo' not in sys.path:
    sys.path.insert(0, '/opt/trn_rl_repo')
import math
import contextlib
import numpy as np
import concourse.bass as bass
import concourse.mybir as mybir
from concourse.bass_utils import run_bass_kernel_spmd

F32 = mybir.dt.float32
BF16 = mybir.dt.bfloat16
I32 = mybir.dt.int32
AF = mybir.ActivationFunctionType
OP = mybir.AluOpType
AX = mybir.AxisListType

DEV = False
if DEV:
    T = 512
    INTER = 1024
else:
    T = 2048
    INTER = 5504

B, D = 8, 2048
H, E = 16, 128
CH = 128
NCHUNK = T // CH
NI = INTER // 128
NK = D // 128
NDC = (NI + 7) // 8
DSL = D // 8
NF_S2 = 0 if DEV else 24
HF_START = 10
TF = T // 2
EPS = 1e-5

SLOPES = [float((2.0 ** (-2.0 ** (-(math.log2(H) - 3)))) ** (i + 1)) for i in range(H)]
BDECAY = [math.exp(-s * CH) for s in SLOPES]

ENGINES = ("sync", "vector", "scalar", "tensor", "gpsimd")
TAGMAP = {}
PROFILE_SCOPES = False


class _Op:
    __slots__ = ("engine", "fn", "reads", "writes", "dma", "tag",
                 "deps", "need_inc", "dslot", "dk", "sreads", "small", "nobar")
    def __init__(self, engine, fn, reads, writes, dma, tag):
        self.engine = engine; self.fn = fn
        self.reads = tuple(reads); self.writes = tuple(writes)
        self.dma = dma; self.tag = tag
        self.deps = []
        self.need_inc = False


def nm(x):
    if isinstance(x, str):
        return x
    t = getattr(x, "tensor", x)
    return t.name


class Prog:
    DMA_POOL = 8

    def __init__(self):
        self.ops = []

    def op(self, engine, fn, reads=(), writes=(), dma=False, tag="", sreads=(),
           small=False, nobar=False):
        o = _Op(engine, fn, [nm(r) for r in reads], [nm(w) for w in writes],
                dma, tag)
        o.sreads = tuple(nm(r) for r in sreads)
        o.small = small
        o.nobar = nobar
        self.ops.append(o)
        return o

    def barrier(self):
        self.ops.append("BARRIER")

    def link_and_emit(self, nc, block):
        last_writer = {}
        readers = {}
        barrier_last = None
        seen_after_barrier = set()
        peo = {e: [] for e in ENGINES}
        POOL = self.DMA_POOL
        dma_counts = {e: 0 for e in ENGINES}
        slot_last = {e: {} for e in ENGINES}
        for item in self.ops:
            if item == "BARRIER":
                barrier_last = {}
                for e in ENGINES:
                    lst = []
                    for o2 in reversed(peo[e]):
                        if not o2.nobar:
                            lst.append(o2)
                            break
                    lst += list(slot_last[e].values())
                    barrier_last[e] = lst
                seen_after_barrier = set()
                continue
            o = item
            if o.dma:
                dn = dma_counts[o.engine]
                o.dslot = dn % POOL
                o.dk = dn // POOL + 1
                dma_counts[o.engine] += 1
                slot_last[o.engine][o.dslot] = o
            deps = []
            if barrier_last is not None and o.engine not in seen_after_barrier:
                seen_after_barrier.add(o.engine)
                for e2, plist in barrier_last.items():
                    if e2 != o.engine:
                        deps.extend(plist)
            hard = set()
            for b in o.reads:
                p = last_writer.get(b)
                if p is not None:
                    deps.append(p)
            for b in o.sreads:
                p = last_writer.get(b)
                if p is not None:
                    deps.append(p)
                    hard.add(id(p))
            for b in o.writes:
                p = last_writer.get(b)
                if p is not None:
                    deps.append(p)
                for r in readers.get(b, ()):
                    deps.append(r)
            dd = {}
            for p in deps:
                if p is o or (p.engine == o.engine and not p.dma
                              and id(p) not in hard and not p.small):
                    continue
                dd[id(p)] = p
            o.deps = list(dd.values())
            for b in list(o.reads) + list(o.sreads):
                readers.setdefault(b, []).append(o)
            for b in o.writes:
                last_writer[b] = o
                readers[b] = []
            peo[o.engine].append(o)

        for item in self.ops:
            if item == "BARRIER":
                continue
            for p in item.deps:
                p.need_inc = True
        for e in ENGINES:
            for o in peo[e]:
                if o.dma:
                    o.need_inc = True
            if peo[e]:
                peo[e][-1].need_inc = True
        inc_rank = {}
        for e in ENGINES:
            c = 0
            for o in peo[e]:
                if o.need_inc and not o.dma:
                    c += 1
                inc_rank[id(o)] = c
        self.final_counts = {}
        for e in ENGINES:
            c = inc_rank[id(peo[e][-1])] if peo[e] else 0
            dfinal = {}
            for o in peo[e]:
                if o.dma:
                    dfinal[o.dslot] = 16 * o.dk
            self.final_counts[e] = (c, dfinal)

        def dep_key(p):
            if p.dma:
                return ("d", p.engine, p.dslot), 16 * p.dk
            return ("c", p.engine), inc_rank[id(p)]

        sim_c = {e: 0 for e in ENGINES}
        sim_d = {(e, s): 0 for e in ENGINES for s in range(POOL)}
        ptrs = {e: 0 for e in ENGINES}
        total = sum(len(v) for v in peo.values())
        done = 0
        while done < total:
            progress = False
            for e in ENGINES:
                while ptrs[e] < len(peo[e]):
                    o = peo[e][ptrs[e]]
                    ok = True
                    for p in o.deps:
                        key, tgt = dep_key(p)
                        cur = sim_d[(key[1], key[2])] if key[0] == "d" else sim_c[key[1]]
                        if cur < tgt:
                            ok = False
                            break
                    if ok and o.dma and o.dk > 1 and sim_d[(e, o.dslot)] < 16 * (o.dk - 1):
                        ok = False
                    if not ok:
                        break
                    if o.dma:
                        sim_d[(e, o.dslot)] = 16 * o.dk
                    elif o.need_inc:
                        sim_c[e] += 1
                    ptrs[e] += 1; done += 1; progress = True
            if not progress:
                stuck = []
                for e in ENGINES:
                    if ptrs[e] < len(peo[e]):
                        o = peo[e][ptrs[e]]
                        blocking = []
                        for p in o.deps:
                            key, tgt = dep_key(p)
                            cur = sim_d[(key[1], key[2])] if key[0] == "d" else sim_c[key[1]]
                            if cur < tgt:
                                blocking.append((p.engine, p.tag))
                        stuck.append((e, o.tag, blocking))
                raise RuntimeError(f"DEADLOCK in schedule: {stuck[:5]}")

        csem = {}; dsem = {}
        stack = contextlib.ExitStack()
        for e in ENGINES:
            csem[e] = stack.enter_context(nc.semaphore(f"c_{e}"))
            dsem[e] = [stack.enter_context(nc.semaphore(f"d_{e}{s}")) for s in range(POOL)]
        self._sem_stack = stack

        def section_for(ename):
            myops = peo[ename]
            def section(eng):
                waited = {}
                def wait(key, tgt, sem):
                    if tgt == 0 or waited.get(key, -1) >= tgt:
                        return
                    eng.wait_ge(sem, tgt)
                    waited[key] = tgt
                for o in myops:
                    for p in o.deps:
                        key, tgt = dep_key(p)
                        sem = dsem[key[1]][key[2]] if key[0] == "d" else csem[key[1]]
                        wait(key, tgt, sem)
                    if o.dma and o.dk > 1:
                        wait(("d", ename, o.dslot), 16 * (o.dk - 1), dsem[ename][o.dslot])
                    if PROFILE_SCOPES:
                        import re as _re
                        ph = _re.match(r"[a-zA-Z]+", o.tag)
                        with nc.named_scope(f"ph_{ph.group(0) if ph else 'x'}"):
                            inst = o.fn(eng)
                    else:
                        inst = o.fn(eng)
                    if o.dma:
                        inst.then_inc(dsem[ename][o.dslot], 16)
                    elif o.need_inc:
                        inst.then_inc(csem[ename], 1)
                if ename == "gpsimd":
                    for e2 in ENGINES:
                        c, dfinal = self.final_counts[e2]
                        if c and e2 != ename:
                            eng.wait_ge(csem[e2], c)
                        for s, v in sorted(dfinal.items()):
                            eng.wait_ge(dsem[e2][s], v)
            return section
        for e in ENGINES:
            if peo[e]:
                getattr(block, e)(section_for(e))
        stats = {e: (len(peo[e]), sum(1 for o in peo[e] if o.dma)) for e in ENGINES}
        return stats



def build_program(debug=()):
    nc = bass.Bass("TRN2", target_bir_lowering=False, detect_race_conditions=False)
    P = Prog()

    x_in = nc.dram_tensor("x", [T, D], F32, kind="ExternalInput")
    wqkv = nc.dram_tensor("wqkv", [3 * D, D], F32, kind="ExternalInput")
    wproj = nc.dram_tensor("wproj", [D, D], F32, kind="ExternalInput")
    theta_in = nc.dram_tensor("theta", [H, 1, E], F32, kind="ExternalInput")
    lut = nc.dram_tensor("lut", [16], F32, kind="ExternalInput")
    gidx_sl = nc.dram_tensor("gidx_sl", [INTER, DSL], I32, kind="ExternalInput")
    uidx_sl = nc.dram_tensor("uidx_sl", [INTER, DSL], I32, kind="ExternalInput")
    didx_sl = nc.dram_tensor("didx_sl", [D, NDC * 128], I32, kind="ExternalInput")
    out = nc.dram_tensor("out", [T, D], F32, kind="ExternalOutput")

    dbg = {}
    def dbg_out(name, shape, dtype=F32):
        t = nc.dram_tensor("dbg_" + name, shape, dtype, kind="ExternalOutput")
        dbg[name] = t
        return t

    NQO = 3 * D // 128
    wprojT_h = nc.dram_tensor("wprojT_h", [NK * 128, D], BF16, kind="Internal")
    y_h = nc.dram_tensor("y_h", [NK * 128, T], BF16, kind="Internal")
    s_h = nc.dram_tensor("s_h", [NK * 128, T], BF16, kind="Internal")
    h_hbm = nc.dram_tensor("h_hbm", [NI * 128, T], BF16, kind="Internal")
    mine_g = nc.dram_tensor("mine_g", [2 * 128, NI * 128], BF16, kind="Internal")
    mine_u = nc.dram_tensor("mine_u", [2 * 128, NI * 128], BF16, kind="Internal")
    mine_d = nc.dram_tensor("mine_d", [NK * 128, NDC * 128], BF16, kind="Internal")
    allg_g = nc.dram_tensor("allg_g", [8 * 2 * 128, NI * 128], BF16, kind="Internal")
    allg_u = nc.dram_tensor("allg_u", [8 * 2 * 128, NI * 128], BF16, kind="Internal")
    allg_d = nc.dram_tensor("allg_d", [8 * NK * 128, NDC * 128], BF16, kind="Internal")

    st_global = contextlib.ExitStack()
    Eg = st_global.enter_context
    SBg = lambda name, shape, dt=BF16: Eg(nc.sbuf_tensor(name, shape, dt))

    consts = SBg("consts", [128, 48], F32)
    iota_f = SBg("iota_f", [128, 128], F32)
    iota_p = SBg("iota_p", [128, 1], F32)
    ident = SBg("ident", [128, 128], BF16)
    lutb = SBg("lutb", [128, 16], F32)
    theta_sb = SBg("theta_sb", [128, H], F32)
    th2pi = SBg("th2pi", [128, H], F32)
    kdec_all = SBg("kdec_all", [128, H], F32)
    st_xn = contextlib.ExitStack()
    xn = [st_xn.enter_context(nc.sbuf_tensor(f"xn{k}", [128, T], BF16)) for k in range(NK)]
    st_tab = contextlib.ExitStack()
    Et = st_tab.enter_context
    iota_t = Et(nc.sbuf_tensor("iota_t", [128, T], F32))
    ddc_rep = Et(nc.sbuf_tensor("ddc_rep", [128, 512], BF16))
    m01_rep = Et(nc.sbuf_tensor("m01_rep", [128, 512], BF16))
    iota_rep = Et(nc.sbuf_tensor("iota_rep", [128, 512], BF16))

    st_p123 = contextlib.ExitStack()
    Ep = st_p123.enter_context
    ptrb = [Ep(nc.psum_tensor("ptr", [128, 1024], BF16)),
            Ep(nc.psum_tensor("ptr2", [128, 1024], BF16))]
    pq = [Ep(nc.psum_tensor(f"pq{i}", [128, 512], F32)) for i in range(4)]
    pf = [Ep(nc.psum_tensor(f"pf{i}", [128, 512], F32)) for i in range(2)]
    _tx = [0]
    def txslot():
        i = _tx[0] % 16
        _tx[0] += 1
        bank = i // 8
        sl = i % 8
        return ptrb[bank][:, sl*128:(sl+1)*128], f"ptrb{bank}"
    _txq = [0]
    def txquad():
        q = _txq[0] % 4
        _txq[0] += 1
        return ptrb[q // 2], (q % 2) * 4, f"ptrb{q // 2}"

    cvals = [0.0, 1.0, 0.25]
    for h in range(H):
        cvals.append(-SLOPES[h])
        cvals.append(-127.0 * SLOPES[h])
    const_ap = {}
    for kk, v in enumerate(cvals):
        nc.const_aps.register(F32, v, consts[:, kk:kk + 1])
        const_ap[v] = consts[:, kk:kk + 1]
        P.op("gpsimd", lambda e, kk=kk, v=v: e.memset(consts[:, kk:kk + 1], v),
             writes=[consts], tag=f"const{kk}")

    def MM(pdst, region, lhsT, rhs, start, stop, extra_reads=(), tag=""):
        reads = [lhsT, rhs] + list(extra_reads)
        if not start:
            reads.append(region)
        P.op("tensor", lambda e, pdst=pdst, lhsT=lhsT, rhs=rhs, start=start, stop=stop:
             e.matmul(pdst, lhsT, rhs, start=start, stop=stop),
             reads=reads, writes=[region], tag=tag)

    def TR(pslot, region, src, tag=""):
        P.op("tensor", lambda e, pslot=pslot, src=src: e.transpose(pslot, src, ident[:]),
             reads=[src, ident], writes=[region], tag=tag)

    P.barrier()
    P.op("gpsimd", lambda e: e.iota(iota_f[:], [[1, 128]], channel_multiplier=0,
                                    allow_small_or_imprecise_dtypes=True),
         writes=[iota_f], tag="iota_f")
    P.op("gpsimd", lambda e: e.iota(iota_p[:], [[0, 1]], channel_multiplier=1,
                                    allow_small_or_imprecise_dtypes=True),
         writes=[iota_p], tag="iota_p")
    P.op("gpsimd", lambda e: e.iota(iota_t[:], [[1, T]], channel_multiplier=0,
                                    allow_small_or_imprecise_dtypes=True),
         writes=[iota_t], tag="iota_t")
    P.op("vector", lambda e: e.tensor_scalar(ident[:], iota_f[:], iota_p[:], None,
                                             op0=OP.is_equal),
         reads=[iota_f], sreads=[iota_p], writes=[ident], tag="ident")
    P.op("sync", lambda e: e.dma_start(out=lutb[:], in_=bass.AP(lut, 0, [[0, 128], [1, 16]])),
         writes=[lutb], dma=True, tag="lut_bcast")
    P.op("sync", lambda e: e.dma_start(out=theta_sb[:], in_=bass.AP(theta_in, 0, [[1, 128], [128, H]]), allow_slow_non_contiguous=True),
         writes=[theta_sb], dma=True, tag="theta_ld")
    P.op("vector", lambda e: e.tensor_scalar(th2pi[:], theta_sb[:], 1.0 / (2 * math.pi), None,
                                             op0=OP.mult),
         sreads=[theta_sb], reads=[], writes=[th2pi], tag="th2pi", small=True)
    dd_f = None
    with contextlib.ExitStack() as s1:
        E1 = s1.enter_context
        dd_f = E1(nc.sbuf_tensor("dd_f", [128, 128], F32))
        ddc_f = E1(nc.sbuf_tensor("ddc_f", [128, 128], F32))
        m01 = E1(nc.sbuf_tensor("m01", [128, 128], BF16))
        em_t = E1(nc.sbuf_tensor("em_t", [128, 128], BF16))
        P.op("vector", lambda e: e.tensor_scalar(dd_f[:], iota_f[:], iota_p[:], None,
                                                 op0=OP.subtract),
             reads=[iota_f], sreads=[iota_p], writes=[dd_f], tag="dd")
        P.op("gpsimd", lambda e: e.iota(iota_rep[:], [[0, 4], [1, 128]], channel_multiplier=0,
                                        allow_small_or_imprecise_dtypes=True),
             writes=[iota_rep], tag="iota_rep")
        P.op("vector", lambda e: e.tensor_scalar(ddc_rep[:], iota_rep[:], iota_p[:], None,
                                                 op0=OP.subtract),
             reads=[iota_rep], sreads=[iota_p], writes=[ddc_rep], tag="ddr")
        P.op("vector", lambda e: e.tensor_scalar(m01_rep[:], ddc_rep[:], 0.0, None, op0=OP.is_ge),
             reads=[ddc_rep], writes=[m01_rep], tag="m01r")
        P.op("vector", lambda e: e.tensor_scalar(ddc_rep[:], ddc_rep[:], 0.0, None, op0=OP.max),
             reads=[ddc_rep], writes=[ddc_rep], tag="ddcr")
        for h in range(H):
            s = SLOPES[h]
            P.op("scalar", lambda e, h=h, s=s: e.activation(
                    kdec_all[:, h:h+1], iota_p[:], AF.Exp, bias=-127.0 * s, scale=s),
                 reads=[iota_p], writes=[kdec_all], tag=f"kdec{h}", small=True)

        x_tile = [E1(nc.sbuf_tensor(f"x_tile{j}", [128, D], F32)) for j in range(2)]
        xn_tm = [E1(nc.sbuf_tensor(f"xn_tm{j}", [128, D], BF16)) for j in range(2)]
        ln_mean = E1(nc.sbuf_tensor("ln_mean", [128, 1], F32))
        ln_var = E1(nc.sbuf_tensor("ln_var", [128, 1], F32))
        ln_rstd = E1(nc.sbuf_tensor("ln_rstd", [128, 1], F32))
        ln_bias = E1(nc.sbuf_tensor("ln_bias", [128, 1], F32))
        sq_scr = E1(nc.sbuf_tensor("sq_scr", [128, D], F32))
        stats_sb = E1(nc.sbuf_tensor("stats_sb", [128, 64], F32))
        if "lnstats" in debug:
            o_st = dbg_out("lnstats", [128, 64], F32)
            P.defer_stats = (o_st, stats_sb)
        for j in range(T // 128):
            buf = x_tile[j % 2]
            tmb = xn_tm[j % 2]
            P.op("sync", lambda e, j=j, buf=buf: e.dma_start(out=buf[:], in_=x_in[j*128:(j+1)*128, :]),
                 writes=[buf], dma=True, tag=f"ln_ld{j}")
            P.op("vector", lambda e, buf=buf: e.reduce_sum(ln_mean[:], buf[:], axis=AX.X),
                 reads=[buf], writes=[ln_mean], tag=f"ln_mean{j}", small=True)
            P.op("scalar", lambda e, buf=buf: e.activation(sq_scr[:], buf[:], AF.Square),
                 reads=[buf], writes=[sq_scr], tag=f"ln_sq{j}")
            P.op("vector", lambda e: e.reduce_sum(ln_var[:], sq_scr[:], axis=AX.X),
                 reads=[sq_scr], writes=[ln_var], tag=f"ln_sqr{j}", small=True)
            P.op("vector", lambda e: e.tensor_scalar(ln_mean[:], ln_mean[:], 1.0 / D, None, op0=OP.mult),
                 reads=[ln_mean], writes=[ln_mean], tag=f"ln_m2{j}", small=True)
            P.op("vector", lambda e: e.tensor_tensor(ln_bias[:], ln_mean[:], ln_mean[:], op=OP.mult),
                 reads=[ln_mean], writes=[ln_bias], tag=f"ln_msq{j}", small=True)
            P.op("vector", lambda e: e.tensor_scalar(ln_var[:], ln_var[:], 1.0 / D, None, op0=OP.mult),
                 reads=[ln_var], writes=[ln_var], tag=f"ln_vm{j}", small=True)
            P.op("vector", lambda e: e.tensor_sub(ln_var[:], ln_var[:], ln_bias[:]),
                 reads=[ln_var, ln_bias], writes=[ln_var], tag=f"ln_v{j}", small=True)
            P.op("vector", lambda e: e.tensor_scalar(ln_var[:], ln_var[:], EPS, None, op0=OP.add),
                 reads=[ln_var], writes=[ln_var], tag=f"ln_veps{j}", small=True)
            P.op("vector", lambda e: e.reciprocal(ln_rstd[:], ln_var[:]),
                 reads=[ln_var], writes=[ln_rstd], tag=f"ln_rec{j}", small=True)
            P.op("scalar", lambda e: e.activation(ln_rstd[:], ln_rstd[:], AF.Sqrt),
                 reads=[ln_rstd], writes=[ln_rstd], tag=f"ln_sqrt{j}", small=True)
            P.op("vector", lambda e: e.tensor_scalar(ln_bias[:], ln_rstd[:], -1.0, None, op0=OP.mult),
                 reads=[ln_rstd], writes=[ln_bias], tag=f"ln_bm{j}", small=True)
            P.op("vector", lambda e: e.tensor_mul(ln_bias[:], ln_bias[:], ln_mean[:]),
                 reads=[ln_bias, ln_mean], writes=[ln_bias], tag=f"ln_b{j}", small=True)
            if "lnstats" in debug:
                for (src_t, off) in ((ln_mean, 0), (ln_var, 1), (ln_rstd, 2), (ln_bias, 3)):
                    P.op("vector", lambda e, src_t=src_t, j=j, off=off: e.tensor_copy(
                            stats_sb[:, 4*j+off:4*j+off+1], src_t[:]),
                         reads=[src_t], writes=[stats_sb], tag=f"lnst{j}_{off}")
            P.op("scalar", lambda e, buf=buf, tmb=tmb: e.activation(tmb[:], buf[:], AF.Identity,
                                                                    bias=ln_bias[:], scale=ln_rstd[:]),
                 reads=[buf], sreads=[ln_rstd, ln_bias], writes=[tmb], tag=f"ln_aff{j}")
            for k in range(NK):
                pslot, bank = txslot()
                TR(pslot, bank, tmb[:, k*128:(k+1)*128], tag=f"ln_tr{j}_{k}")
                P.op("vector", lambda e, k=k, j=j, pslot=pslot:
                     e.tensor_copy(xn[k][:, j*128:(j+1)*128], pslot),
                     reads=[bank], writes=[xn[k]], tag=f"ln_ev{j}_{k}")

        if "lnstats" in debug:
            o_st, _sb = P.defer_stats
            P.op("gpsimd", lambda e, o_st=o_st, _sb=_sb: e.dma_start(out=o_st.ap(), in_=_sb[:]),
                 reads=[_sb], dma=True, tag="dbg_lnstats")
        if "xn" in debug:
            o_ = dbg_out("xn", [NK * 128, T], BF16)
            for k in range(NK):
                P.op("gpsimd", lambda e, k=k, o_=o_: e.dma_start(out=o_[k*128:(k+1)*128, :], in_=xn[k][:]),
                     reads=[xn[k]], dma=True, tag=f"dbg_xn{k}")

    if "s1only" in debug:
        return nc, P, st_global, dbg
    P.barrier()

    gather_units = []
    coll_marks = []

    def emit_gather_plane(which, src, mine, kk, bufs):
        (gi_sb3, gc_sb3, cbfT, gacc, gtmp2, _gt) = bufs
        N = NI

        NQ = (N + 3) // 4

        def nt(q):
            return min(4, N - q * 4)

        def ld(q):
            gi_sb = gi_sb3[q % 2]
            for t in range(nt(q)):
                i = q * 4 + t
                P.op("sync", lambda e, i=i, t=t, gi_sb=gi_sb: e.dma_start(
                        out=gi_sb[:, t*128:(t+1)*128],
                        in_=src[i*128:(i+1)*128, kk*128:(kk+1)*128]),
                     writes=[gi_sb], dma=True, tag=f"g{which}{kk}_ld{i}")

        def cast(q):
            gi_sb = gi_sb3[q % 2]; gc_sb = gc_sb3[q % 2]
            w = nt(q) * 128
            P.op("scalar", lambda e, gi_sb=gi_sb, gc_sb=gc_sb, w=w: e.copy(
                    gc_sb[:, 0:w], gi_sb[:, 0:w]),
                 reads=[gi_sb], writes=[gc_sb], tag=f"g{which}{kk}_cast{q}")

        def trev(q):
            gc_sb = gc_sb3[q % 2]
            bank4, base, bnm = txquad()
            for t in range(nt(q)):
                TR(bank4[:, (base+t)*128:(base+t+1)*128], bnm,
                   gc_sb[:, t*128:(t+1)*128], tag=f"g{which}{kk}_tr{q}_{t}")
            w = nt(q) * 128
            P.op("scalar", lambda e, q=q, bank4=bank4, base=base, w=w: e.copy(
                    cbfT[:, q*512:q*512+w], bank4[:, base*128:base*128+w]),
                 reads=[bnm], writes=[cbfT], tag=f"g{which}{kk}_ev{q}")

        def upro():
            ld(0); cast(0)
        gather_units.append(upro)
        for q in range(NQ):
            def ua(q=q):
                if q + 1 < NQ:
                    ld(q + 1)
                    cast(q + 1)
                trev(q)
            gather_units.append(ua)
        W = N * 128
        Wh = W - W // 2
        for hw in range(2):
            c0 = hw * Wh
            c1 = min(W, c0 + Wh)
            def u0(c0=c0, c1=c1):
                P.op("vector", lambda e, c0=c0, c1=c1: e.tensor_scalar(
                        gacc[:, c0:c1], cbfT[:, c0:c1], 0.0, lutb[:, 0:1],
                        op0=OP.is_equal, op1=OP.mult),
                     reads=[cbfT], sreads=[lutb], writes=[gacc], tag=f"g{which}{kk}_ts0_{hw}")
            gather_units.append(u0)
            for j in range(1, 16):
                def uj(j=j, c0=c0, c1=c1, hw=hw):
                    gt = gtmp2[j % 2]; gtn = f"gtmp{j % 2}"
                    P.op("vector", lambda e, j=j, c0=c0, c1=c1, gt=gt: e.tensor_scalar(
                            gt[:, 0:c1-c0], cbfT[:, c0:c1], float(j), lutb[:, j:j+1],
                            op0=OP.is_equal, op1=OP.mult),
                         reads=[cbfT], sreads=[lutb], writes=[gtn], tag=f"g{which}{kk}_ts{j}_{hw}")
                    P.op("vector", lambda e, j=j, c0=c0, c1=c1, gt=gt: e.tensor_add(
                            gacc[:, c0:c1], gacc[:, c0:c1], gt[:, 0:c1-c0]),
                         reads=[gacc, gtn], writes=[gacc], tag=f"g{which}{kk}_add{j}_{hw}")
                gather_units.append(uj)

        def ust():
            P.op("scalar", lambda e: e.dma_start(
                    out=mine[kk*128:(kk+1)*128, :], in_=gacc[:, 0:W]),
                 reads=[gacc], writes=[mine], dma=True, tag=f"g{which}{kk}_st")
        gather_units.append(ust)

    def emit_gather_dslab(s, bufs):
        (gi_sb3, gc_sb3, cbfT, gacc, gtmp2, _gt) = bufs
        N = NK

        NQ = N // 4

        def ld(q):
            gi_sb = gi_sb3[q % 2]
            for t in range(4):
                o2 = q * 4 + t
                P.op("sync", lambda e, o2=o2, t=t, s=s, gi_sb=gi_sb: e.dma_start(
                        out=gi_sb[:, t*128:(t+1)*128],
                        in_=didx_sl[o2*128:(o2+1)*128, s*128:(s+1)*128]),
                     writes=[gi_sb], dma=True, tag=f"gd{s}_ld{o2}")

        def cast(q):
            gi_sb = gi_sb3[q % 2]; gc_sb = gc_sb3[q % 2]
            P.op("scalar", lambda e, gi_sb=gi_sb, gc_sb=gc_sb: e.copy(gc_sb[:], gi_sb[:]),
                 reads=[gi_sb], writes=[gc_sb], tag=f"gd{s}_cast{q}")

        def trev(q):
            gc_sb = gc_sb3[q % 2]
            bank4, base, bnm = txquad()
            for t in range(4):
                TR(bank4[:, (base+t)*128:(base+t+1)*128], bnm,
                   gc_sb[:, t*128:(t+1)*128], tag=f"gd{s}_tr{q}_{t}")
            P.op("scalar", lambda e, q=q, bank4=bank4, base=base: e.copy(
                    cbfT[:, q*512:(q+1)*512, ], bank4[:, base*128:(base+4)*128]),
                 reads=[bnm], writes=[cbfT], tag=f"gd{s}_ev{q}")

        def upro():
            ld(0); cast(0)
        gather_units.append(upro)
        for q in range(NQ):
            def ua(q=q):
                if q + 1 < NQ:
                    ld(q + 1)
                    cast(q + 1)
                trev(q)
            gather_units.append(ua)
        for hw in range(2):
            c0 = hw * (D // 2); c1 = c0 + D // 2
            def u0(c0=c0, c1=c1):
                P.op("vector", lambda e, c0=c0, c1=c1: e.tensor_scalar(
                        gacc[:, c0:c1], cbfT[:, c0:c1], 0.0, lutb[:, 0:1],
                        op0=OP.is_equal, op1=OP.mult),
                     reads=[cbfT], sreads=[lutb], writes=[gacc], tag=f"gd{s}_ts0_{hw}")
            gather_units.append(u0)
            for j0 in range(1, 16, 2):
                def uj(j0=j0, c0=c0, c1=c1, hw=hw):
                    for j in range(j0, min(16, j0 + 2)):
                        gt = gtmp2[j % 2]; gtn = f"gtmp{j % 2}"
                        P.op("vector", lambda e, j=j, c0=c0, c1=c1, gt=gt: e.tensor_scalar(
                                gt[:, 0:D//2], cbfT[:, c0:c1], float(j), lutb[:, j:j+1],
                                op0=OP.is_equal, op1=OP.mult),
                             reads=[cbfT], sreads=[lutb], writes=[gtn], tag=f"gd{s}_ts{j}_{hw}")
                        P.op("vector", lambda e, j=j, c0=c0, c1=c1, gt=gt: e.tensor_add(
                                gacc[:, c0:c1], gacc[:, c0:c1], gt[:, 0:D//2]),
                             reads=[gacc, gtn], writes=[gacc], tag=f"gd{s}_add{j}_{hw}")
                gather_units.append(uj)

        def ust():
            for o2 in range(NK):
                P.op("scalar", lambda e, o2=o2, s=s: e.dma_start(
                        out=mine_d[o2*128:(o2+1)*128, s*128:(s+1)*128],
                        in_=gacc[:, o2*128:(o2+1)*128]),
                     reads=[gacc], writes=[mine_d], dma=True, tag=f"gd{s}_st{o2}")
        gather_units.append(ust)

    with contextlib.ExitStack() as s2:
        E2 = s2.enter_context
        gwide = max(NI * 128, D)
        TH_ = T // 4
        scr_a = E2(nc.sbuf_tensor("scr_a", [128, TH_], F32))
        scr_c = E2(nc.sbuf_tensor("scr_c", [128, TH_], F32))
        scr_i = E2(nc.sbuf_tensor("scr_i", [128, TH_], I32))
        _gt = [0]
        wq_sb = [E2(nc.sbuf_tensor(f"wq_sb{j}", [128, D], BF16)) for j in range(2)]
        wk_sb = [E2(nc.sbuf_tensor(f"wk_sb{j}", [128, D], BF16)) for j in range(2)]
        wv_sb = [E2(nc.sbuf_tensor(f"wv_sb{j}", [128, D], BF16)) for j in range(2)]
        q_fm = E2(nc.sbuf_tensor("q_fm", [128, T], BF16))
        k_fm = E2(nc.sbuf_tensor("k_fm", [128, T], BF16))
        v_fm = E2(nc.sbuf_tensor("v_fm", [128, T], BF16))
        cos_h = E2(nc.sbuf_tensor("cos_h", [128, T], BF16))
        sin_h = E2(nc.sbuf_tensor("sin_h", [128, T], BF16))
        qh2 = [E2(nc.sbuf_tensor(f"qh2_{j}", [128, T], BF16)) for j in range(2)]
        kh2 = [E2(nc.sbuf_tensor(f"kh2_{j}", [128, T], BF16)) for j in range(2)]
        ktm = E2(nc.sbuf_tensor("ktm", [128, NCHUNK * 256], BF16))
        vtm = E2(nc.sbuf_tensor("vtm", [128, NCHUNK * 128], BF16))
        sT4_sb = E2(nc.sbuf_tensor("sT4_sb", [128, 512], BF16))
        qt4_sb = E2(nc.sbuf_tensor("qt4_sb", [128, 1024], BF16))
        st_f = E2(nc.sbuf_tensor("st_f", [128, 256], F32))
        st_bf4 = E2(nc.sbuf_tensor("st_bf4", [128, 1024], BF16))
        dmask4 = E2(nc.sbuf_tensor("dmask4", [128, 512], BF16))
        qdec4 = E2(nc.sbuf_tensor("qdec4", [128, 512], BF16))
        y_stg = [E2(nc.sbuf_tensor("y_stg0", [128, T], BF16))] * 2

        f_units = []
        f_i = [0]
        f_bufs = {}

        def setup_F():
            wfg = E2(nc.sbuf_tensor("wfg", [128, NK * 256], BF16))
            wfu = E2(nc.sbuf_tensor("wfu", [128, NK * 256], BF16))
            f_gsil = E2(nc.sbuf_tensor("f_gsil", [128, T], BF16))
            f_ht = [E2(nc.sbuf_tensor(f"f_ht{j}", [128, T], BF16)) for j in range(2)]

            def f_load(p, role):
                wbuf, wbn, agg = (wfg, "wfg", allg_g) if role == "g" else (wfu, "wfu", allg_u)
                for k in range(NK):
                    c2, kk2 = k // 2, k % 2
                    row = (c2 * 2 + kk2) * 128
                    P.op("sync" if k % 2 == 0 else "scalar",
                         lambda e, row=row, k=k, wbuf=wbuf, p=p, agg=agg: e.dma_start(
                            out=wbuf[:, k*256:(k+1)*256],
                            in_=agg[row:row+128, p*256:(p+1)*256]),
                         reads=[agg], writes=[wbn], dma=True, tag=f"fs{role}w{p}_{k}")

            def f_g_unit(i, tc):
                t = i % 2
                for k in range(NK):
                    MM(pq[2][:], "pq2", wfg[:, k*256+t*128:k*256+t*128+128],
                       xn[k][:, tc*512:(tc+1)*512], start=(k == 0), stop=(k == NK - 1),
                       extra_reads=[xn[k], "wfg"], tag=f"fsg{i}_mm{tc}_{k}")
                P.op("scalar", lambda e, tc=tc: e.activation(
                        f_gsil[:, tc*512:(tc+1)*512], pq[2][:], AF.Silu),
                     reads=["pq2"], writes=[f_gsil], tag=f"fsg{i}_sil{tc}")

            def f_u_unit(i, tc):
                t = i % 2
                ht = f_ht[i % 2]
                for k in range(NK):
                    MM(pq[2][:], "pq2", wfu[:, k*256+t*128:k*256+t*128+128],
                       xn[k][:, tc*512:(tc+1)*512], start=(k == 0), stop=(k == NK - 1),
                       extra_reads=[xn[k], "wfu"], tag=f"fsu{i}_mm{tc}_{k}")
                P.op("vector", lambda e, tc=tc, ht=ht: e.tensor_tensor(
                        ht[:, tc*512:(tc+1)*512], f_gsil[:, tc*512:(tc+1)*512],
                        pq[2][:], op=OP.mult),
                     reads=[f_gsil, "pq2"], writes=[ht], tag=f"fsu{i}_ht{tc}")
                if tc == 3:
                    P.op("gpsimd", lambda e, i=i, ht=ht: e.dma_start(
                            out=h_hbm[i*128:(i+1)*128, :], in_=ht[:]),
                         reads=[ht], writes=[h_hbm], dma=True, tag=f"fsh{i}_st")

            NP_ = NF_S2 // 2
            for p_ in range(NP_):
                if p_ == 0:
                    f_units.append(lambda: (f_load(0, "g"), f_load(0, "u")))
                for t_ in range(2):
                    i_ = 2 * p_ + t_
                    for tc_ in range(4):
                        f_units.append(lambda i_=i_, tc_=tc_: f_g_unit(i_, tc_))
                    if t_ == 1 and p_ + 1 < NP_:
                        f_units.append(lambda p_=p_: f_load(p_ + 1, "g"))
                    for tc_ in range(4):
                        f_units.append(lambda i_=i_, tc_=tc_: f_u_unit(i_, tc_))
                    if t_ == 1 and p_ + 1 < NP_:
                        f_units.append(lambda p_=p_: f_load(p_ + 1, "u"))

        def emit_f_progress(n):
            for _ in range(n):
                if f_i[0] >= len(f_units):
                    return
                f_units[f_i[0]]()
                f_i[0] += 1

        NTC = T // 512 if T >= 512 else 1
        TCW = min(512, T)

        wrowf2 = [E2(nc.sbuf_tensor(f"wrowf{j}", [128, D // 2], F32)) for j in range(2)]
        ident_f = E2(nc.sbuf_tensor("ident_f", [128, 128], F32))
        P.op("vector", lambda e: e.tensor_copy(ident_f[:], ident[:]),
             reads=[ident], writes=[ident_f], tag="identf")
        _wt = [0]

        st_g = contextlib.ExitStack()
        Egt = st_g.enter_context
        gi_sb3 = [Egt(nc.sbuf_tensor(f"gi_sb{j}", [128, 512], I32)) for j in range(2)]
        gc_sb3 = [Egt(nc.sbuf_tensor(f"gc_sb{j}", [128, 512], BF16)) for j in range(2)]
        cbfT = Egt(nc.sbuf_tensor("cbfT", [128, gwide], BF16))
        gacc = Egt(nc.sbuf_tensor("gacc", [128, gwide], BF16))
        gtmp2 = [Egt(nc.sbuf_tensor("gtmp0", [128, gwide // 2], BF16))] * 2
        gbufs = (gi_sb3, gc_sb3, cbfT, gacc, gtmp2, _gt)
        emit_gather_plane("g", gidx_sl, mine_g, 0, gbufs)
        emit_gather_plane("g", gidx_sl, mine_g, 1, gbufs)
        coll_marks.append((len(gather_units), "cc_g", mine_g, allg_g))
        emit_gather_plane("u", uidx_sl, mine_u, 0, gbufs)
        emit_gather_plane("u", uidx_sl, mine_u, 1, gbufs)
        coll_marks.append((len(gather_units), "cc_u", mine_u, allg_u))
        for s in range(NDC):
            emit_gather_dslab(s, gbufs)
        coll_marks.append((len(gather_units), "cc_d", mine_d, allg_d))
        gb_i = 0

        def emit_gather_progress(n):
            nonlocal gb_i
            for _ in range(n):
                if gb_i >= len(gather_units):
                    return
                gather_units[gb_i]()
                gb_i += 1
                for (at, tag, mine, allg) in coll_marks:
                    if at == gb_i:
                        P.op("gpsimd", lambda e, mine=mine, allg=allg: e.collective_compute(
                                "AllGather", OP.bypass, replica_groups=[list(range(8))],
                                ins=[mine.ap().opt()], outs=[allg.ap().opt()]),
                             reads=[mine], writes=[allg], tag=tag, nobar=True)


        def emit_wT(orow_global, dstbuf, dst_hbm, tag):
            srcten = wqkv if orow_global < 3 * NK else wproj
            ro = orow_global if orow_global < 3 * NK else orow_global - 3 * NK
            buf = dstbuf if dstbuf is not None else kh2[1]
            for half in range(2):
                wrowf = wrowf2[_wt[0] % 2]
                _wt[0] += 1
                P.op("sync", lambda e, ro=ro, srcten=srcten, wrowf=wrowf, half=half:
                     e.dma_start(out=wrowf[:],
                                 in_=srcten[ro*128:(ro+1)*128, half*(D//2):(half+1)*(D//2)]),
                     writes=[wrowf], dma=True, tag=f"{tag}_ldf{half}")
                for kx in range(NK // 2):
                    k = half * (NK // 2) + kx
                    pfb = pq[0]
                    bank = "pq0"
                    pslot = pfb[:, (k % 4)*128:(k % 4 + 1)*128]
                    P.op("tensor", lambda e, pslot=pslot, wrowf=wrowf, kx=kx:
                         e.transpose(pslot, wrowf[:, kx*128:(kx+1)*128], ident_f[:]),
                         reads=[wrowf, ident_f], writes=[bank], tag=f"{tag}_tr{k}")
                    if k % 2 == 0:
                        P.op("scalar", lambda e, k=k, buf=buf, pslot=pslot:
                             e.copy(buf[:, k*128:(k+1)*128], pslot),
                             reads=[bank], writes=[buf], tag=f"{tag}_ev{k}")
                    else:
                        P.op("vector", lambda e, k=k, buf=buf, pslot=pslot:
                             e.tensor_copy(buf[:, k*128:(k+1)*128], pslot),
                             reads=[bank], writes=[buf], tag=f"{tag}_ev{k}")
            if dst_hbm is not None:
                P.op("sync", lambda e, dst_hbm=dst_hbm, buf=buf: e.dma_start(
                        out=dst_hbm, in_=buf[:]),
                     reads=[buf], writes=[wprojT_h], dma=True, tag=f"{tag}_st")

        def emit_sincos_unit(h, hf_, off, dst, part):
            if part == 0:
                P.op("vector", lambda e, h=h, hf_=hf_: e.tensor_scalar(
                        scr_a[:], iota_t[:, hf_*TH_:(hf_+1)*TH_], th2pi[:, h:h+1], None, op0=OP.mult),
                     reads=[iota_t], sreads=[th2pi], writes=[scr_a], tag=f"h{h}_u{hf_}{off}")
                if off:
                    P.op("vector", lambda e: e.tensor_scalar(
                            scr_a[:], scr_a[:], 0.25, None, op0=OP.add),
                         reads=[scr_a], writes=[scr_a], tag=f"h{h}_uo{hf_}")
                P.op("vector", lambda e: e.tensor_copy(scr_i[:], scr_a[:]),
                     reads=[scr_a], writes=[scr_i], tag=f"h{h}_i{hf_}{off}")
            elif part == 1:
                P.op("vector", lambda e: e.tensor_copy(scr_c[:], scr_i[:]),
                     reads=[scr_i], writes=[scr_c], tag=f"h{h}_f{hf_}{off}")
                P.op("vector", lambda e: e.tensor_sub(scr_c[:], scr_a[:], scr_c[:]),
                     reads=[scr_a, scr_c], writes=[scr_c], tag=f"h{h}_r{hf_}{off}")
            else:
                P.op("vector", lambda e: e.scalar_tensor_tensor(scr_c[:], scr_c[:], 0.5,
                        scr_c[:], op0=OP.is_gt, op1=OP.subtract),
                     reads=[scr_c], writes=[scr_c], tag=f"h{h}_w1{hf_}{off}")
                P.op("vector", lambda e: e.scalar_tensor_tensor(scr_c[:], scr_c[:], 0.5,
                        scr_c[:], op0=OP.is_gt, op1=OP.subtract),
                     reads=[scr_c], writes=[scr_c], tag=f"h{h}_w2{hf_}{off}")
                P.op("scalar", lambda e, dst=dst, hf_=hf_: e.activation(
                        dst[:, hf_*TH_:(hf_+1)*TH_], scr_c[:], AF.Sin, scale=2 * math.pi),
                     reads=[scr_c], writes=[dst], tag=f"h{h}_a{hf_}{off}")

        def emit_qkv_unit(h, wt, dstf, nmq, tc):
            pb = pq[0]
            pbn = "pq0"
            for k in range(NK):
                MM(pb[:, 0:TCW], pbn, wt[:, k*128:(k+1)*128],
                   xn[k][:, tc*TCW:(tc+1)*TCW], start=(k == 0), stop=(k == NK - 1),
                   tag=f"h{h}_{nmq}mm{k}_{tc}")
            P.op("scalar", lambda e, dstf=dstf, tc=tc, pb=pb: e.copy(
                    dstf[:, tc*TCW:(tc+1)*TCW], pb[:, 0:TCW]),
                 reads=[pbn], writes=[dstf], tag=f"h{h}_{nmq}ev{tc}")

        def A_units(h):
            wq, wk, wv = wq_sb[h % 2], wk_sb[h % 2], wv_sb[h % 2]
            units = []
            units.append(lambda: emit_wT(h, wq, None, f"h{h}_wq"))
            for tc in range(NTC):
                units.append(lambda tc=tc, wq=wq: emit_qkv_unit(h, wq, q_fm, "q", tc))
            units.append(lambda: emit_wT(NK + h, wk, None, f"h{h}_wk"))
            for tc in range(NTC):
                units.append(lambda tc=tc, wk=wk: emit_qkv_unit(h, wk, k_fm, "k", tc))
            units.append(lambda: emit_wT(2 * NK + h, wv, None, f"h{h}_wv"))
            for tc in range(NTC):
                units.append(lambda tc=tc, wv=wv: emit_qkv_unit(h, wv, v_fm, "v", tc))
            if h >= H - 8:
                for j_ in range(2):
                    o = (h - (H - 8)) * 2 + j_
                    units.append(lambda o=o: emit_wT(3 * NK + o, None,
                                                     wprojT_h[o*128:(o+1)*128, :], f"wp{o}"))
            for hf_ in range(4):
                for part in range(3):
                    units.append(lambda hf_=hf_, part=part: emit_sincos_unit(h, hf_, 0.0, sin_h, part))
                for part in range(3):
                    units.append(lambda hf_=hf_, part=part: emit_sincos_unit(h, hf_, 0.25, cos_h, part))
            return units

        def B_pre(h):
            s = SLOPES[h]
            P.op("scalar", lambda e, s=s: e.activation(
                    qdec4[:], iota_rep[:], AF.Exp, bias=-s, scale=-s),
                 reads=[iota_rep], writes=[qdec4], tag=f"h{h}_qdec")
            P.op("scalar", lambda e, s=s: e.activation(
                    dmask4[:], ddc_rep[:], AF.Exp, scale=-s),
                 reads=[ddc_rep], writes=[dmask4], tag=f"h{h}_em")
            P.op("vector", lambda e: e.tensor_mul(dmask4[:], dmask4[:], m01_rep[:]),
                 reads=[dmask4, m01_rep], writes=[dmask4], tag=f"h{h}_dm")
            P.op("vector", lambda e: e.tensor_mul(qh2[0][:], q_fm[:], cos_h[:]),
                 reads=[q_fm, cos_h], writes=[qh2[0]], tag=f"h{h}_lrq0")
            P.op("vector", lambda e: e.tensor_mul(qh2[1][:], q_fm[:], sin_h[:]),
                 reads=[q_fm, sin_h], writes=[qh2[1]], tag=f"h{h}_lrq1")
            P.op("vector", lambda e: e.tensor_mul(kh2[0][:], k_fm[:], cos_h[:]),
                 reads=[k_fm, cos_h], writes=[kh2[0]], tag=f"h{h}_lrk0")
            P.op("vector", lambda e: e.tensor_mul(kh2[1][:], k_fm[:], sin_h[:]),
                 reads=[k_fm, sin_h], writes=[kh2[1]], tag=f"h{h}_lrk1")
            for ip in range(NCHUNK // 2):
                bank4, base, bnm = txquad()
                for half4 in range(4):
                    i = ip * 2 + half4 // 2
                    half = half4 % 2
                    TR(bank4[:, (base+half4)*128:(base+half4+1)*128], bnm,
                       kh2[half][:, i*128:(i+1)*128], tag=f"h{h}_ktr{ip}_{half4}")
                P.op("scalar", lambda e, ip=ip, h=h, bank4=bank4, base=base: e.mul(
                        ktm[:, ip*512:(ip+1)*512], bank4[:, base*128:(base+4)*128],
                        kdec_all[:, h:h+1]),
                     reads=[bnm], sreads=[kdec_all], writes=[ktm], tag=f"h{h}_kev{ip}")
            for ip in range(NCHUNK // 4):
                bank4, base, bnm = txquad()
                for j4 in range(4):
                    i = ip * 4 + j4
                    TR(bank4[:, (base+j4)*128:(base+j4+1)*128], bnm,
                       v_fm[:, i*128:(i+1)*128], tag=f"h{h}_vtr{ip}_{j4}")
                P.op("scalar", lambda e, ip=ip, bank4=bank4, base=base: e.copy(
                        vtm[:, ip*512:(ip+1)*512], bank4[:, base*128:(base+4)*128]),
                     reads=[bnm], writes=[vtm], tag=f"h{h}_vev{ip}")

        def C1(h, sc):
            sccols = slice(sc * 512, (sc + 1) * 512)
            for j in range(4):
                i = sc * 4 + j
                cols = slice(i * 128, (i + 1) * 128)
                MM(pf[0][:, j*128:(j+1)*128], "pf0", kh2[0][:, cols], qh2[0][:, cols],
                   True, False, tag=f"h{h}_c{i}_s0")
                MM(pf[0][:, j*128:(j+1)*128], "pf0", kh2[1][:, cols], qh2[1][:, cols],
                   False, True, tag=f"h{h}_c{i}_s1")
            P.op("vector", lambda e: e.tensor_tensor(
                    sT4_sb[:], pf[0][:], dmask4[:], op=OP.mult),
                 reads=["pf0", dmask4], writes=[sT4_sb], tag=f"h{h}_sc{sc}_mask")
            P.op("vector", lambda e, sccols=sccols: e.tensor_tensor(
                    qt4_sb[:, 0:512], qh2[0][:, sccols], qdec4[:], op=OP.mult),
                 reads=[qh2[0], qdec4], writes=[qt4_sb], tag=f"h{h}_sc{sc}_qt0")
            P.op("vector", lambda e, sccols=sccols: e.tensor_tensor(
                    qt4_sb[:, 512:1024], qh2[1][:, sccols], qdec4[:], op=OP.mult),
                 reads=[qh2[1], qdec4], writes=[qt4_sb], tag=f"h{h}_sc{sc}_qt1")
            for j in range(4):
                i = sc * 4 + j
                if i >= NCHUNK - 1:
                    continue
                cols = slice(i * 128, (i + 1) * 128)
                pdl = pq[1] if j < 2 else pf[0]
                pdn = "pq1" if j < 2 else "pf0"
                co = (j % 2) * 256
                MM(pdl[:, co:co+128], pdn, ktm[:, i*256:i*256+128], vtm[:, cols],
                   True, True, tag=f"h{h}_c{i}_d0")
                MM(pdl[:, co+128:co+256], pdn, ktm[:, i*256+128:i*256+256], vtm[:, cols],
                   True, True, tag=f"h{h}_c{i}_d1")

        def C2(h, sc, ystg):
            po = pq[3] if sc % 2 == 0 else pf[1]
            po_n = "pq3" if sc % 2 == 0 else "pf1"
            def upd(j):
                i = sc * 4 + j
                if i >= NCHUNK - 1:
                    return
                pdl = pq[1] if j < 2 else pf[0]
                pdn = "pq1" if j < 2 else "pf0"
                co = (j % 2) * 256
                if i == 0:
                    P.op("vector", lambda e, pdl=pdl, co=co: e.tensor_copy(
                            st_f[:], pdl[:, co:co+256]),
                         reads=[pdn], writes=[st_f], tag=f"h{h}_c{i}_st01")
                else:
                    bd = BDECAY[h]
                    P.op("vector", lambda e, bd=bd, pdl=pdl, co=co: e.scalar_tensor_tensor(
                            st_f[:], st_f[:], bd, pdl[:, co:co+256], op0=OP.mult, op1=OP.add),
                         reads=[st_f, pdn], writes=[st_f], tag=f"h{h}_c{i}_sa")
                slot = ((i + 1) % 4) * 256
                P.op("vector", lambda e, slot=slot: e.tensor_copy(
                        st_bf4[:, slot:slot+256], st_f[:]),
                     reads=[st_f], writes=[st_bf4], tag=f"h{h}_c{i}_stbf")
            for j in range(3):
                upd(j)
            first = True
            for j in range(4):
                i = sc * 4 + j
                cols = slice(i * 128, (i + 1) * 128)
                last = (j == 3) and (sc * 4 == 0 and False)
                MM(po[:, j*128:(j+1)*128], po_n, vtm[:, cols], sT4_sb[:, j*128:(j+1)*128],
                   first, False, tag=f"h{h}_c{i}_o0")
                first = False
            nmm = []
            for j in range(4):
                i = sc * 4 + j
                if i == 0:
                    continue
                slot = (i % 4) * 256
                nmm.append((j, slot))
            for idx, (j, slot) in enumerate(nmm):
                i = sc * 4 + j
                stop = (idx == len(nmm) - 1)
                MM(po[:, j*128:(j+1)*128], po_n, st_bf4[:, slot:slot+128],
                   qt4_sb[:, j*128:(j+1)*128], False, False, tag=f"h{h}_c{i}_o1")
                MM(po[:, j*128:(j+1)*128], po_n, st_bf4[:, slot+128:slot+256],
                   qt4_sb[:, 512+j*128:512+(j+1)*128], False, stop, tag=f"h{h}_c{i}_o2")
            if not nmm:
                pass
            P.op("scalar", lambda e, sc=sc, ystg=ystg, po=po: e.copy(
                    ystg[:, sc*512:(sc+1)*512], po[:]),
                 reads=[po_n], writes=[ystg], tag=f"h{h}_sc{sc}_yev")
            upd(3)

        for h in range(H + 1):
            if h == HF_START:
                emit_gather_progress(10**9)
                st_g.close()
                P.barrier()
                if NF_S2:
                    setup_F()
            units = A_units(h) if h < H else []
            ai = 0
            if h >= 1:
                ystg = y_stg[(h - 1) % 2]
                for sc in range(NCHUNK // 4):
                    C1(h - 1, sc)
                    for _ in range(6):
                        if ai < len(units):
                            units[ai](); ai += 1
                    C2(h - 1, sc, ystg)
                    for _ in range(5):
                        if ai < len(units):
                            units[ai](); ai += 1
                    if h < HF_START:
                        emit_gather_progress(9)
                    else:
                        emit_f_progress(8)
                P.op("scalar", lambda e, h=h, ystg=ystg: e.dma_start(
                        out=y_h[(h-1)*128:h*128, :], in_=ystg[:]),
                     reads=[ystg], writes=[y_h], dma=True, tag=f"h{h-1}_yst")
            while ai < len(units):
                units[ai](); ai += 1
            if h == 0:
                emit_gather_progress(16)
            if h < H:
                B_pre(h)
        emit_gather_progress(10**9)
        emit_f_progress(10**9)

        if "y" in debug:
            o_ = dbg_out("y", [NK * 128, T], BF16)
            P.op("gpsimd", lambda e, o_=o_: e.dma_start(out=o_.ap(), in_=y_h.ap()),
                 reads=["y_h"], dma=True, tag="dbg_y")
        if "gather" in debug:
            om = dbg_out("mine_d", [NK * 128, NDC * 128], BF16)
            P.op("gpsimd", lambda e, om=om: e.dma_start(out=om.ap(), in_=mine_d.ap()),
                 reads=[mine_d], dma=True, tag="dbg_md")
            o_ = dbg_out("allg_g", [8 * 2 * 128, NI * 128], BF16)
            P.op("gpsimd", lambda e, o_=o_: e.dma_start(out=o_.ap(), in_=allg_g.ap()),
                 reads=[allg_g], dma=True, tag="dbg_gg")
            o2_ = dbg_out("allg_d", [8 * NK * 128, NDC * 128], BF16)
            P.op("gpsimd", lambda e, o2_=o2_: e.dma_start(out=o2_.ap(), in_=allg_d.ap()),
                 reads=[allg_d], dma=True, tag="dbg_gd")

    P.barrier()

    st_tab.close()
    with contextlib.ExitStack() as s3:
        E3 = s3.enter_context
        y_sb = [E3(nc.sbuf_tensor(f"y_sb{k}", [128, TCW], BF16)) for k in range(NK)]
        wpass = [E3(nc.sbuf_tensor(f"wpass{j}", [128, D], BF16)) for j in range(8)]
        s_sb = [E3(nc.sbuf_tensor(f"s_sb{j}", [128, TCW], BF16)) for j in range(4)]
        for tc in range(NTC):
            for k in range(NK):
                P.op("sync", lambda e, k=k, tc=tc: e.dma_start(
                        out=y_sb[k][:], in_=y_h[k*128:(k+1)*128, tc*TCW:(tc+1)*TCW]),
                     writes=[y_sb[k]], dma=True, tag=f"p{tc}_yld{k}")
            for oq in range(4):
                for o4 in range(4):
                    o = oq * 4 + o4
                    wp = wpass[(oq % 2) * 4 + o4]
                    P.op("sync", lambda e, o=o, wp=wp: e.dma_start(
                            out=wp[:], in_=wprojT_h[o*128:(o+1)*128, :]),
                         writes=[wp], dma=True, tag=f"p{tc}_{oq}_wld{o4}")
                    for k in range(NK):
                        MM(pq[o4][:, 0:TCW], f"pq{o4}", wp[:, k*128:(k+1)*128], y_sb[k][:],
                           start=(k == 0), stop=(k == NK - 1), tag=f"p{tc}_{oq}_mm{o4}_{k}")
                    P.op("vector", lambda e, o4=o4: e.tensor_copy(s_sb[o4][:], pq[o4][:, 0:TCW]),
                         reads=[f"pq{o4}"], writes=[s_sb[o4]], tag=f"p{tc}_{oq}_ev{o4}")
                    P.op("scalar", lambda e, o=o, tc=tc, o4=o4: e.dma_start(
                            out=s_h[o*128:(o+1)*128, tc*TCW:(tc+1)*TCW], in_=s_sb[o4][:]),
                         reads=[s_sb[o4]], writes=[s_h], dma=True, tag=f"p{tc}_{oq}_st{o4}")
        if "s" in debug:
            o_ = dbg_out("s", [NK * 128, T], BF16)
            P.op("gpsimd", lambda e, o_=o_: e.dma_start(out=o_.ap(), in_=s_h.ap()),
                 reads=["s_h"], dma=True, tag="dbg_s")

    P.barrier()

    st_p123.close()
    NQUAD = (NI + 3) // 4
    NTC4 = T // 512
    with contextlib.ExitStack() as s4a:
        E4 = s4a.enter_context
        pg = [E4(nc.psum_tensor(f"pg{t}", [128, 512], F32)) for t in range(NTC4)]
        pu = [E4(nc.psum_tensor(f"pu{t}", [128, 512], F32)) for t in range(NTC4)]
        wg_win = [E4(nc.sbuf_tensor(f"wg_win{j}", [128, NK * 512], BF16)) for j in range(2)]
        wu_win = [E4(nc.sbuf_tensor(f"wu_win{j}", [128, NK * 512], BF16)) for j in range(2)]
        gsil = E4(nc.sbuf_tensor("gsil", [128, T], BF16))
        h_t = [E4(nc.sbuf_tensor(f"h_t{j}", [128, T], BF16)) for j in range(2)]
        for iq in range(NF_S2 // 4, NQUAD):
            qn = min(4, NI - iq * 4)
            wgw = wg_win[iq % 2]; wuw = wu_win[iq % 2]
            wc0, wc1 = iq * 512, iq * 512 + qn * 128
            for k in range(NK):
                c, kk = k // 2, k % 2
                row = (c * 2 + kk) * 128
                P.op("sync", lambda e, k=k, row=row, wgw=wgw, wc0=wc0, wc1=wc1, qn=qn:
                     e.dma_start(out=wgw[:, k*512:k*512+qn*128], in_=allg_g[row:row+128, wc0:wc1]),
                     reads=[allg_g], writes=[wgw], dma=True, tag=f"fg_w{iq}_{k}")
                P.op("scalar", lambda e, k=k, row=row, wuw=wuw, wc0=wc0, wc1=wc1, qn=qn:
                     e.dma_start(out=wuw[:, k*512:k*512+qn*128], in_=allg_u[row:row+128, wc0:wc1]),
                     reads=[allg_u], writes=[wuw], dma=True, tag=f"fu_w{iq}_{k}")
            for q in range(qn):
                i = iq * 4 + q
                ht = h_t[i % 2]
                for k in range(NK):
                    wsl = wgw[:, k*512+q*128:k*512+(q+1)*128]
                    for tc in range(NTC4):
                        MM(pg[tc][:], f"pg{tc}", wsl, xn[k][:, tc*512:(tc+1)*512],
                           start=(k == 0), stop=(k == NK - 1), extra_reads=[xn[k]],
                           tag=f"fg{i}_mm{k}_{tc}")
                for tc in range(NTC4):
                    P.op("scalar", lambda e, tc=tc: e.activation(
                            gsil[:, tc*512:(tc+1)*512], pg[tc][:], AF.Silu),
                         reads=[f"pg{tc}"], writes=[gsil], tag=f"fg{i}_sil{tc}")
                for k in range(NK):
                    wsl = wuw[:, k*512+q*128:k*512+(q+1)*128]
                    for tc in range(NTC4):
                        MM(pu[tc][:], f"pu{tc}", wsl, xn[k][:, tc*512:(tc+1)*512],
                           start=(k == 0), stop=(k == NK - 1), extra_reads=[xn[k]],
                           tag=f"fu{i}_mm{k}_{tc}")
                for tc in range(NTC4):
                    P.op("vector", lambda e, tc=tc, ht=ht: e.tensor_tensor(
                            ht[:, tc*512:(tc+1)*512], gsil[:, tc*512:(tc+1)*512],
                            pu[tc][:], op=OP.mult),
                         reads=[gsil, f"pu{tc}"], writes=[ht], tag=f"fh{i}_{tc}")
                P.op("sync", lambda e, i=i, ht=ht: e.dma_start(
                        out=h_hbm[i*128:(i+1)*128, :], in_=ht[:]),
                     reads=[ht], writes=[h_hbm], dma=True, tag=f"fh{i}_st")

    if "h" in debug:
        oh = dbg_out("h", [NI * 128, T], BF16)
        P.op("gpsimd", lambda e, oh=oh: e.dma_start(out=oh.ap(), in_=h_hbm.ap()),
             reads=[h_hbm], dma=True, tag="dbg_h")
    st_xn.close()
    P.barrier()

    with contextlib.ExitStack() as s4b:
        E4 = s4b.enter_context
        pd = [E4(nc.psum_tensor(f"pd{j}", [128, 512], F32)) for j in range(4)]
        ptr3 = [E4(nc.psum_tensor("ptr3", [128, 1024], BF16)),
                E4(nc.psum_tensor("ptr4", [128, 1024], BF16))]
        _tx4 = [0]
        def txslot4():
            i = _tx4[0] % 16
            _tx4[0] += 1
            return ptr3[i // 8][:, (i % 8)*128:(i % 8)*128+128], f"ptr34{i // 8}"
        hslab = E4(nc.sbuf_tensor("hslab", [128, NI * 1024], BF16))
        wd_sb = [E4(nc.sbuf_tensor(f"wd_sb{j}", [128, 8 * NDC * 128], BF16)) for j in range(2)]
        g_sb = E4(nc.sbuf_tensor("g_sb", [128, 1024], BF16))
        s_t = [E4(nc.sbuf_tensor(f"s_t{j}", [128, 1024], BF16)) for j in range(2)]
        sg_sb = E4(nc.sbuf_tensor("sg_sb", [128, 1024], BF16))
        sg_tm = [E4(nc.sbuf_tensor(f"sg_tm{b}", [128, D], BF16)) for b in range(8)]
        xrow = [E4(nc.sbuf_tensor(f"xrow{j}", [128, D], F32)) for j in range(2)]
        orow = [E4(nc.sbuf_tensor(f"orow{j}", [128, D], F32)) for j in range(2)]
        for tc in range(2):
            tcsl = slice(tc * 1024, (tc + 1) * 1024)
            for i in range(NI):
                P.op("sync" if i % 2 == 0 else "scalar",
                     lambda e, i=i, tcsl=tcsl: e.dma_start(
                        out=hslab[:, i*1024:(i+1)*1024], in_=h_hbm[i*128:(i+1)*128, tcsl]),
                     reads=[h_hbm], writes=["hslab"], dma=True, tag=f"fd{tc}_h{i}")
            for o2 in range(NK):
                wd = wd_sb[o2 % 2]
                pda = pd[(o2 % 2) * 2]; pdb = pd[(o2 % 2) * 2 + 1]
                pda_n = f"pd{(o2 % 2) * 2}"; pdb_n = f"pd{(o2 % 2) * 2 + 1}"
                for c in range(8):
                    P.op("sync", lambda e, c=c, o2=o2, wd=wd: e.dma_start(
                            out=wd[:, c*NDC*128:(c+1)*NDC*128],
                            in_=allg_d[(c*NK + o2)*128:(c*NK + o2 + 1)*128, :]),
                         reads=[allg_d], writes=[wd], dma=True, tag=f"fd{tc}_{o2}_w{c}")
                for ci in range(NI):
                    c = ci % 8; s = ci // 8
                    wcol = (c * NDC + s) * 128
                    MM(pda[:], pda_n, wd[:, wcol:wcol+128], hslab[:, ci*1024:ci*1024+512],
                       start=(ci == 0), stop=(ci == NI - 1), extra_reads=["hslab"],
                       tag=f"fd{tc}_{o2}_mmA{ci}")
                    MM(pdb[:], pdb_n, wd[:, wcol:wcol+128], hslab[:, ci*1024+512:(ci+1)*1024],
                       start=(ci == 0), stop=(ci == NI - 1), extra_reads=["hslab"],
                       tag=f"fd{tc}_{o2}_mmB{ci}")
                st_ = s_t[o2 % 2]
                P.op("scalar", lambda e, o2=o2, tcsl=tcsl, st_=st_: e.dma_start(
                        out=st_[:], in_=s_h[o2*128:(o2+1)*128, tcsl]),
                     reads=[s_h], writes=[st_], dma=True, tag=f"fd{tc}_{o2}_sld")
                P.op("scalar", lambda e, pda=pda: e.activation(g_sb[:, 0:512], pda[:], AF.Silu),
                     reads=[pda_n], writes=[g_sb], tag=f"fd{tc}_{o2}_silA")
                P.op("scalar", lambda e, pdb=pdb: e.activation(g_sb[:, 512:1024], pdb[:], AF.Silu),
                     reads=[pdb_n], writes=[g_sb], tag=f"fd{tc}_{o2}_silB")
                P.op("vector", lambda e, st_=st_: e.tensor_mul(sg_sb[:], g_sb[:], st_[:]),
                     reads=[g_sb, st_], writes=[sg_sb], tag=f"fd{tc}_{o2}_sg")
                for b in range(8):
                    pslot, bank = txslot4()
                    TR(pslot, bank, sg_sb[:, b*128:(b+1)*128], tag=f"fd{tc}_{o2}_tr{b}")
                    P.op("vector", lambda e, b=b, o2=o2, pslot=pslot: e.tensor_copy(
                            sg_tm[b][:, o2*128:(o2+1)*128], pslot),
                         reads=[bank], writes=[sg_tm[b]], tag=f"fd{tc}_{o2}_tev{b}")
            for b in range(8):
                r = tc * 8 + b
                xr = xrow[b % 2]; orr = orow[b % 2]
                P.op("sync", lambda e, r=r, xr=xr: e.dma_start(
                        out=xr[:], in_=x_in[r*128:(r+1)*128, :]),
                     writes=[xr], dma=True, tag=f"o{r}_x")
                P.op("vector", lambda e, xr=xr, orr=orr, b=b: e.tensor_tensor(
                        orr[:], xr[:], sg_tm[b][:], op=OP.add),
                     reads=[xr, sg_tm[b]], writes=[orr], tag=f"o{r}_add")
                P.op("scalar", lambda e, r=r, orr=orr: e.dma_start(
                        out=out[r*128:(r+1)*128, :], in_=orr[:]),
                     reads=[orr], dma=True, tag=f"o{r}_st")

    return nc, P, st_global, dbg


def finalize(nc, P):
    with nc.Block() as block:
        stats = P.link_and_emit(nc, block)
    return nc, stats



def shard_inputs(inputs):
    x = np.ascontiguousarray(inputs["L0_attn_x"])
    gidx = inputs["L0_ffn_gate_idx"]; uidx = inputs["L0_ffn_up_idx"]
    didx = inputs["L0_ffn_down_idx"]
    maps = []
    for c in range(8):
        cis = [ci for ci in range(c, NI, 8)]
        while len(cis) < NDC:
            cis.append(NI - 1)
        dsl = np.concatenate([didx[:, ci*128:(ci+1)*128] for ci in cis], axis=1)
        maps.append({
            "x": x[c],
            "wqkv": np.ascontiguousarray(inputs["L0_attn_w_qkv"]),
            "wproj": np.ascontiguousarray(inputs["L0_attn_w_proj"]),
            "theta": np.ascontiguousarray(inputs["L0_attn_theta"]),
            "lut": np.ascontiguousarray(inputs["L0_ffn_lut"]),
            "gidx_sl": np.ascontiguousarray(gidx[:, DSL*c:DSL*(c+1)]),
            "uidx_sl": np.ascontiguousarray(uidx[:, DSL*c:DSL*(c+1)]),
            "didx_sl": np.ascontiguousarray(dsl),
        })
    return maps


_CACHE = {}

def kernel(L0_attn_x, L0_attn_w_qkv, L0_attn_w_proj, L0_attn_theta,
           L0_ffn_lut, L0_ffn_gate_idx, L0_ffn_up_idx, L0_ffn_down_idx):
    inputs = dict(L0_attn_x=np.asarray(L0_attn_x),
                  L0_attn_w_qkv=np.asarray(L0_attn_w_qkv),
                  L0_attn_w_proj=np.asarray(L0_attn_w_proj),
                  L0_attn_theta=np.asarray(L0_attn_theta),
                  L0_ffn_lut=np.asarray(L0_ffn_lut),
                  L0_ffn_gate_idx=np.asarray(L0_ffn_gate_idx),
                  L0_ffn_up_idx=np.asarray(L0_ffn_up_idx),
                  L0_ffn_down_idx=np.asarray(L0_ffn_down_idx))
    if "nc" not in _CACHE:
        nc, P, st, dbg = build_program()
        nc, stats = finalize(nc, P)
        _CACHE["nc"] = nc
    nc = _CACHE["nc"]
    maps = shard_inputs(inputs)
    res = run_bass_kernel_spmd(nc, maps, core_ids=list(range(8)))
    return np.stack([res.results[c]["out"] for c in range(8)], axis=0)




# version 19 (final)
# speedup vs baseline: 1.1516x; 1.1516x over previous
import sys
if '/opt/trn_rl_repo' not in sys.path:
    sys.path.insert(0, '/opt/trn_rl_repo')
import math
import contextlib
import numpy as np
import concourse.bass as bass
import concourse.mybir as mybir
from concourse.bass_utils import run_bass_kernel_spmd

F32 = mybir.dt.float32
BF16 = mybir.dt.bfloat16
I32 = mybir.dt.int32
AF = mybir.ActivationFunctionType
OP = mybir.AluOpType
AX = mybir.AxisListType

DEV = False
if DEV:
    T = 512
    INTER = 1024
else:
    T = 2048
    INTER = 5504

B, D = 8, 2048
H, E = 16, 128
CH = 128
NCHUNK = T // CH
NI = INTER // 128
NK = D // 128
NDC = (NI + 7) // 8
DSL = D // 8
TF = T // 2
EPS = 1e-5

SLOPES = [float((2.0 ** (-2.0 ** (-(math.log2(H) - 3)))) ** (i + 1)) for i in range(H)]
BDECAY = [math.exp(-s * CH) for s in SLOPES]

ENGINES = ("sync", "vector", "scalar", "tensor", "gpsimd")
TAGMAP = {}
PROFILE_SCOPES = False


class _Op:
    __slots__ = ("engine", "fn", "reads", "writes", "dma", "tag",
                 "deps", "need_inc", "dslot", "dk", "sreads", "small", "nobar")
    def __init__(self, engine, fn, reads, writes, dma, tag):
        self.engine = engine; self.fn = fn
        self.reads = tuple(reads); self.writes = tuple(writes)
        self.dma = dma; self.tag = tag
        self.deps = []
        self.need_inc = False


def nm(x):
    if isinstance(x, str):
        return x
    t = getattr(x, "tensor", x)
    return t.name


class Prog:
    DMA_POOL = 8

    def __init__(self):
        self.ops = []

    def op(self, engine, fn, reads=(), writes=(), dma=False, tag="", sreads=(),
           small=False, nobar=False):
        o = _Op(engine, fn, [nm(r) for r in reads], [nm(w) for w in writes],
                dma, tag)
        o.sreads = tuple(nm(r) for r in sreads)
        o.small = small
        o.nobar = nobar
        self.ops.append(o)
        return o

    def barrier(self):
        self.ops.append("BARRIER")

    def link_and_emit(self, nc, block):
        last_writer = {}
        readers = {}
        barrier_last = None
        seen_after_barrier = set()
        peo = {e: [] for e in ENGINES}
        POOL = self.DMA_POOL
        dma_counts = {e: 0 for e in ENGINES}
        slot_last = {e: {} for e in ENGINES}
        for item in self.ops:
            if item == "BARRIER":
                barrier_last = {}
                for e in ENGINES:
                    lst = []
                    for o2 in reversed(peo[e]):
                        if not o2.nobar:
                            lst.append(o2)
                            break
                    lst += list(slot_last[e].values())
                    barrier_last[e] = lst
                seen_after_barrier = set()
                continue
            o = item
            if o.dma:
                dn = dma_counts[o.engine]
                o.dslot = dn % POOL
                o.dk = dn // POOL + 1
                dma_counts[o.engine] += 1
                slot_last[o.engine][o.dslot] = o
            deps = []
            if barrier_last is not None and o.engine not in seen_after_barrier:
                seen_after_barrier.add(o.engine)
                for e2, plist in barrier_last.items():
                    if e2 != o.engine:
                        deps.extend(plist)
            hard = set()
            for b in o.reads:
                p = last_writer.get(b)
                if p is not None:
                    deps.append(p)
            for b in o.sreads:
                p = last_writer.get(b)
                if p is not None:
                    deps.append(p)
                    hard.add(id(p))
            for b in o.writes:
                p = last_writer.get(b)
                if p is not None:
                    deps.append(p)
                for r in readers.get(b, ()):
                    deps.append(r)
            dd = {}
            for p in deps:
                if p is o or (p.engine == o.engine and not p.dma
                              and id(p) not in hard and not p.small):
                    continue
                dd[id(p)] = p
            o.deps = list(dd.values())
            for b in list(o.reads) + list(o.sreads):
                readers.setdefault(b, []).append(o)
            for b in o.writes:
                last_writer[b] = o
                readers[b] = []
            peo[o.engine].append(o)

        for item in self.ops:
            if item == "BARRIER":
                continue
            for p in item.deps:
                p.need_inc = True
        for e in ENGINES:
            for o in peo[e]:
                if o.dma:
                    o.need_inc = True
            if peo[e]:
                peo[e][-1].need_inc = True
        inc_rank = {}
        for e in ENGINES:
            c = 0
            for o in peo[e]:
                if o.need_inc and not o.dma:
                    c += 1
                inc_rank[id(o)] = c
        self.final_counts = {}
        for e in ENGINES:
            c = inc_rank[id(peo[e][-1])] if peo[e] else 0
            dfinal = {}
            for o in peo[e]:
                if o.dma:
                    dfinal[o.dslot] = 16 * o.dk
            self.final_counts[e] = (c, dfinal)

        def dep_key(p):
            if p.dma:
                return ("d", p.engine, p.dslot), 16 * p.dk
            return ("c", p.engine), inc_rank[id(p)]

        sim_c = {e: 0 for e in ENGINES}
        sim_d = {(e, s): 0 for e in ENGINES for s in range(POOL)}
        ptrs = {e: 0 for e in ENGINES}
        total = sum(len(v) for v in peo.values())
        done = 0
        while done < total:
            progress = False
            for e in ENGINES:
                while ptrs[e] < len(peo[e]):
                    o = peo[e][ptrs[e]]
                    ok = True
                    for p in o.deps:
                        key, tgt = dep_key(p)
                        cur = sim_d[(key[1], key[2])] if key[0] == "d" else sim_c[key[1]]
                        if cur < tgt:
                            ok = False
                            break
                    if ok and o.dma and o.dk > 1 and sim_d[(e, o.dslot)] < 16 * (o.dk - 1):
                        ok = False
                    if not ok:
                        break
                    if o.dma:
                        sim_d[(e, o.dslot)] = 16 * o.dk
                    elif o.need_inc:
                        sim_c[e] += 1
                    ptrs[e] += 1; done += 1; progress = True
            if not progress:
                stuck = []
                for e in ENGINES:
                    if ptrs[e] < len(peo[e]):
                        o = peo[e][ptrs[e]]
                        blocking = []
                        for p in o.deps:
                            key, tgt = dep_key(p)
                            cur = sim_d[(key[1], key[2])] if key[0] == "d" else sim_c[key[1]]
                            if cur < tgt:
                                blocking.append((p.engine, p.tag))
                        stuck.append((e, o.tag, blocking))
                raise RuntimeError(f"DEADLOCK in schedule: {stuck[:5]}")

        csem = {}; dsem = {}
        stack = contextlib.ExitStack()
        for e in ENGINES:
            csem[e] = stack.enter_context(nc.semaphore(f"c_{e}"))
            dsem[e] = [stack.enter_context(nc.semaphore(f"d_{e}{s}")) for s in range(POOL)]
        self._sem_stack = stack

        def section_for(ename):
            myops = peo[ename]
            def section(eng):
                waited = {}
                def wait(key, tgt, sem):
                    if tgt == 0 or waited.get(key, -1) >= tgt:
                        return
                    eng.wait_ge(sem, tgt)
                    waited[key] = tgt
                for o in myops:
                    for p in o.deps:
                        key, tgt = dep_key(p)
                        sem = dsem[key[1]][key[2]] if key[0] == "d" else csem[key[1]]
                        wait(key, tgt, sem)
                    if o.dma and o.dk > 1:
                        wait(("d", ename, o.dslot), 16 * (o.dk - 1), dsem[ename][o.dslot])
                    if PROFILE_SCOPES:
                        import re as _re
                        ph = _re.match(r"[a-zA-Z]+", o.tag)
                        with nc.named_scope(f"ph_{ph.group(0) if ph else 'x'}"):
                            inst = o.fn(eng)
                    else:
                        inst = o.fn(eng)
                    if o.dma:
                        inst.then_inc(dsem[ename][o.dslot], 16)
                    elif o.need_inc:
                        inst.then_inc(csem[ename], 1)
                if ename == "gpsimd":
                    for e2 in ENGINES:
                        c, dfinal = self.final_counts[e2]
                        if c and e2 != ename:
                            eng.wait_ge(csem[e2], c)
                        for s, v in sorted(dfinal.items()):
                            eng.wait_ge(dsem[e2][s], v)
            return section
        for e in ENGINES:
            if peo[e]:
                getattr(block, e)(section_for(e))
        stats = {e: (len(peo[e]), sum(1 for o in peo[e] if o.dma)) for e in ENGINES}
        return stats



def build_program(debug=()):
    nc = bass.Bass("TRN2", target_bir_lowering=False, detect_race_conditions=False)
    P = Prog()

    x_in = nc.dram_tensor("x", [T, D], F32, kind="ExternalInput")
    wqkv = nc.dram_tensor("wqkv", [3 * D, D], F32, kind="ExternalInput")
    wproj = nc.dram_tensor("wproj", [D, D], F32, kind="ExternalInput")
    theta_in = nc.dram_tensor("theta", [H, 1, E], F32, kind="ExternalInput")
    lut = nc.dram_tensor("lut", [16], F32, kind="ExternalInput")
    gidx_sl = nc.dram_tensor("gidx_sl", [INTER, DSL], I32, kind="ExternalInput")
    uidx_sl = nc.dram_tensor("uidx_sl", [INTER, DSL], I32, kind="ExternalInput")
    didx_sl = nc.dram_tensor("didx_sl", [D, NDC * 128], I32, kind="ExternalInput")
    out = nc.dram_tensor("out", [T, D], F32, kind="ExternalOutput")

    dbg = {}
    def dbg_out(name, shape, dtype=F32):
        t = nc.dram_tensor("dbg_" + name, shape, dtype, kind="ExternalOutput")
        dbg[name] = t
        return t

    NQO = 3 * D // 128
    wprojT_h = nc.dram_tensor("wprojT_h", [NK * 128, D], BF16, kind="Internal")
    y_h = nc.dram_tensor("y_h", [NK * 128, T], BF16, kind="Internal")
    s_h = nc.dram_tensor("s_h", [NK * 128, T], BF16, kind="Internal")
    h_hbm = nc.dram_tensor("h_hbm", [NI * 128, T], BF16, kind="Internal")
    mine_g = nc.dram_tensor("mine_g", [2 * 128, NI * 128], BF16, kind="Internal")
    mine_u = nc.dram_tensor("mine_u", [2 * 128, NI * 128], BF16, kind="Internal")
    mine_d = nc.dram_tensor("mine_d", [NK * 128, NDC * 128], BF16, kind="Internal")
    allg_g = nc.dram_tensor("allg_g", [8 * 2 * 128, NI * 128], BF16, kind="Internal")
    allg_u = nc.dram_tensor("allg_u", [8 * 2 * 128, NI * 128], BF16, kind="Internal")
    allg_d = nc.dram_tensor("allg_d", [8 * NK * 128, NDC * 128], BF16, kind="Internal")

    st_global = contextlib.ExitStack()
    Eg = st_global.enter_context
    SBg = lambda name, shape, dt=BF16: Eg(nc.sbuf_tensor(name, shape, dt))

    consts = SBg("consts", [128, 48], F32)
    iota_f = SBg("iota_f", [128, 128], F32)
    iota_p = SBg("iota_p", [128, 1], F32)
    ident = SBg("ident", [128, 128], BF16)
    lutb = SBg("lutb", [128, 16], F32)
    theta_sb = SBg("theta_sb", [128, H], F32)
    th2pi = SBg("th2pi", [128, H], F32)
    kdec_all = SBg("kdec_all", [128, H], F32)
    st_xn = contextlib.ExitStack()
    xn = [st_xn.enter_context(nc.sbuf_tensor(f"xn{k}", [128, T], BF16)) for k in range(NK)]
    st_tab = contextlib.ExitStack()
    Et = st_tab.enter_context
    iota_t = Et(nc.sbuf_tensor("iota_t", [128, T], F32))
    ddc_rep = Et(nc.sbuf_tensor("ddc_rep", [128, 512], BF16))
    m01_rep = Et(nc.sbuf_tensor("m01_rep", [128, 512], BF16))
    iota_rep = Et(nc.sbuf_tensor("iota_rep", [128, 512], BF16))

    st_p123 = contextlib.ExitStack()
    Ep = st_p123.enter_context
    ptrb = [Ep(nc.psum_tensor("ptr", [128, 1024], BF16)),
            Ep(nc.psum_tensor("ptr2", [128, 1024], BF16))]
    pq = [Ep(nc.psum_tensor(f"pq{i}", [128, 512], F32)) for i in range(4)]
    pf = [Ep(nc.psum_tensor(f"pf{i}", [128, 512], F32)) for i in range(2)]
    _tx = [0]
    def txslot():
        i = _tx[0] % 16
        _tx[0] += 1
        bank = i // 8
        sl = i % 8
        return ptrb[bank][:, sl*128:(sl+1)*128], f"ptrb{bank}"
    _txq = [0]
    def txquad():
        q = _txq[0] % 4
        _txq[0] += 1
        return ptrb[q // 2], (q % 2) * 4, f"ptrb{q // 2}"

    cvals = [0.0, 1.0, 0.25]
    for h in range(H):
        cvals.append(-SLOPES[h])
        cvals.append(-127.0 * SLOPES[h])
    const_ap = {}
    for kk, v in enumerate(cvals):
        nc.const_aps.register(F32, v, consts[:, kk:kk + 1])
        const_ap[v] = consts[:, kk:kk + 1]
        P.op("gpsimd", lambda e, kk=kk, v=v: e.memset(consts[:, kk:kk + 1], v),
             writes=[consts], tag=f"const{kk}")

    def MM(pdst, region, lhsT, rhs, start, stop, extra_reads=(), tag=""):
        reads = [lhsT, rhs] + list(extra_reads)
        if not start:
            reads.append(region)
        P.op("tensor", lambda e, pdst=pdst, lhsT=lhsT, rhs=rhs, start=start, stop=stop:
             e.matmul(pdst, lhsT, rhs, start=start, stop=stop),
             reads=reads, writes=[region], tag=tag)

    def TR(pslot, region, src, tag=""):
        P.op("tensor", lambda e, pslot=pslot, src=src: e.transpose(pslot, src, ident[:]),
             reads=[src, ident], writes=[region], tag=tag)

    P.barrier()
    P.op("gpsimd", lambda e: e.iota(iota_f[:], [[1, 128]], channel_multiplier=0,
                                    allow_small_or_imprecise_dtypes=True),
         writes=[iota_f], tag="iota_f")
    P.op("gpsimd", lambda e: e.iota(iota_p[:], [[0, 1]], channel_multiplier=1,
                                    allow_small_or_imprecise_dtypes=True),
         writes=[iota_p], tag="iota_p")
    P.op("gpsimd", lambda e: e.iota(iota_t[:], [[1, T]], channel_multiplier=0,
                                    allow_small_or_imprecise_dtypes=True),
         writes=[iota_t], tag="iota_t")
    P.op("vector", lambda e: e.tensor_scalar(ident[:], iota_f[:], iota_p[:], None,
                                             op0=OP.is_equal),
         reads=[iota_f], sreads=[iota_p], writes=[ident], tag="ident")
    P.op("sync", lambda e: e.dma_start(out=lutb[:], in_=bass.AP(lut, 0, [[0, 128], [1, 16]])),
         writes=[lutb], dma=True, tag="lut_bcast")
    P.op("sync", lambda e: e.dma_start(out=theta_sb[:], in_=bass.AP(theta_in, 0, [[1, 128], [128, H]]), allow_slow_non_contiguous=True),
         writes=[theta_sb], dma=True, tag="theta_ld")
    P.op("vector", lambda e: e.tensor_scalar(th2pi[:], theta_sb[:], 1.0 / (2 * math.pi), None,
                                             op0=OP.mult),
         sreads=[theta_sb], reads=[], writes=[th2pi], tag="th2pi", small=True)
    dd_f = None
    with contextlib.ExitStack() as s1:
        E1 = s1.enter_context
        dd_f = E1(nc.sbuf_tensor("dd_f", [128, 128], F32))
        ddc_f = E1(nc.sbuf_tensor("ddc_f", [128, 128], F32))
        m01 = E1(nc.sbuf_tensor("m01", [128, 128], BF16))
        em_t = E1(nc.sbuf_tensor("em_t", [128, 128], BF16))
        P.op("vector", lambda e: e.tensor_scalar(dd_f[:], iota_f[:], iota_p[:], None,
                                                 op0=OP.subtract),
             reads=[iota_f], sreads=[iota_p], writes=[dd_f], tag="dd")
        P.op("gpsimd", lambda e: e.iota(iota_rep[:], [[0, 4], [1, 128]], channel_multiplier=0,
                                        allow_small_or_imprecise_dtypes=True),
             writes=[iota_rep], tag="iota_rep")
        P.op("vector", lambda e: e.tensor_scalar(ddc_rep[:], iota_rep[:], iota_p[:], None,
                                                 op0=OP.subtract),
             reads=[iota_rep], sreads=[iota_p], writes=[ddc_rep], tag="ddr")
        P.op("vector", lambda e: e.tensor_scalar(m01_rep[:], ddc_rep[:], 0.0, None, op0=OP.is_ge),
             reads=[ddc_rep], writes=[m01_rep], tag="m01r")
        P.op("vector", lambda e: e.tensor_scalar(ddc_rep[:], ddc_rep[:], 0.0, None, op0=OP.max),
             reads=[ddc_rep], writes=[ddc_rep], tag="ddcr")
        for h in range(H):
            s = SLOPES[h]
            P.op("scalar", lambda e, h=h, s=s: e.activation(
                    kdec_all[:, h:h+1], iota_p[:], AF.Exp, bias=-127.0 * s, scale=s),
                 reads=[iota_p], writes=[kdec_all], tag=f"kdec{h}", small=True)

        x_tile = [E1(nc.sbuf_tensor(f"x_tile{j}", [128, D], F32)) for j in range(2)]
        xn_tm = [E1(nc.sbuf_tensor(f"xn_tm{j}", [128, D], BF16)) for j in range(2)]
        ln_mean = E1(nc.sbuf_tensor("ln_mean", [128, 1], F32))
        ln_var = E1(nc.sbuf_tensor("ln_var", [128, 1], F32))
        ln_rstd = E1(nc.sbuf_tensor("ln_rstd", [128, 1], F32))
        ln_bias = E1(nc.sbuf_tensor("ln_bias", [128, 1], F32))
        sq_scr = E1(nc.sbuf_tensor("sq_scr", [128, D], F32))
        stats_sb = E1(nc.sbuf_tensor("stats_sb", [128, 64], F32))
        if "lnstats" in debug:
            o_st = dbg_out("lnstats", [128, 64], F32)
            P.defer_stats = (o_st, stats_sb)
        for j in range(T // 128):
            buf = x_tile[j % 2]
            tmb = xn_tm[j % 2]
            P.op("sync", lambda e, j=j, buf=buf: e.dma_start(out=buf[:], in_=x_in[j*128:(j+1)*128, :]),
                 writes=[buf], dma=True, tag=f"ln_ld{j}")
            P.op("vector", lambda e, buf=buf: e.reduce_sum(ln_mean[:], buf[:], axis=AX.X),
                 reads=[buf], writes=[ln_mean], tag=f"ln_mean{j}", small=True)
            P.op("scalar", lambda e, buf=buf: e.activation(sq_scr[:], buf[:], AF.Square),
                 reads=[buf], writes=[sq_scr], tag=f"ln_sq{j}")
            P.op("vector", lambda e: e.reduce_sum(ln_var[:], sq_scr[:], axis=AX.X),
                 reads=[sq_scr], writes=[ln_var], tag=f"ln_sqr{j}", small=True)
            P.op("vector", lambda e: e.tensor_scalar(ln_mean[:], ln_mean[:], 1.0 / D, None, op0=OP.mult),
                 reads=[ln_mean], writes=[ln_mean], tag=f"ln_m2{j}", small=True)
            P.op("vector", lambda e: e.tensor_tensor(ln_bias[:], ln_mean[:], ln_mean[:], op=OP.mult),
                 reads=[ln_mean], writes=[ln_bias], tag=f"ln_msq{j}", small=True)
            P.op("vector", lambda e: e.tensor_scalar(ln_var[:], ln_var[:], 1.0 / D, None, op0=OP.mult),
                 reads=[ln_var], writes=[ln_var], tag=f"ln_vm{j}", small=True)
            P.op("vector", lambda e: e.tensor_sub(ln_var[:], ln_var[:], ln_bias[:]),
                 reads=[ln_var, ln_bias], writes=[ln_var], tag=f"ln_v{j}", small=True)
            P.op("vector", lambda e: e.tensor_scalar(ln_var[:], ln_var[:], EPS, None, op0=OP.add),
                 reads=[ln_var], writes=[ln_var], tag=f"ln_veps{j}", small=True)
            P.op("vector", lambda e: e.reciprocal(ln_rstd[:], ln_var[:]),
                 reads=[ln_var], writes=[ln_rstd], tag=f"ln_rec{j}", small=True)
            P.op("scalar", lambda e: e.activation(ln_rstd[:], ln_rstd[:], AF.Sqrt),
                 reads=[ln_rstd], writes=[ln_rstd], tag=f"ln_sqrt{j}", small=True)
            P.op("vector", lambda e: e.tensor_scalar(ln_bias[:], ln_rstd[:], -1.0, None, op0=OP.mult),
                 reads=[ln_rstd], writes=[ln_bias], tag=f"ln_bm{j}", small=True)
            P.op("vector", lambda e: e.tensor_mul(ln_bias[:], ln_bias[:], ln_mean[:]),
                 reads=[ln_bias, ln_mean], writes=[ln_bias], tag=f"ln_b{j}", small=True)
            if "lnstats" in debug:
                for (src_t, off) in ((ln_mean, 0), (ln_var, 1), (ln_rstd, 2), (ln_bias, 3)):
                    P.op("vector", lambda e, src_t=src_t, j=j, off=off: e.tensor_copy(
                            stats_sb[:, 4*j+off:4*j+off+1], src_t[:]),
                         reads=[src_t], writes=[stats_sb], tag=f"lnst{j}_{off}")
            P.op("scalar", lambda e, buf=buf, tmb=tmb: e.activation(tmb[:], buf[:], AF.Identity,
                                                                    bias=ln_bias[:], scale=ln_rstd[:]),
                 reads=[buf], sreads=[ln_rstd, ln_bias], writes=[tmb], tag=f"ln_aff{j}")
            for k in range(NK):
                pslot, bank = txslot()
                TR(pslot, bank, tmb[:, k*128:(k+1)*128], tag=f"ln_tr{j}_{k}")
                P.op("vector", lambda e, k=k, j=j, pslot=pslot:
                     e.tensor_copy(xn[k][:, j*128:(j+1)*128], pslot),
                     reads=[bank], writes=[xn[k]], tag=f"ln_ev{j}_{k}")

        if "lnstats" in debug:
            o_st, _sb = P.defer_stats
            P.op("gpsimd", lambda e, o_st=o_st, _sb=_sb: e.dma_start(out=o_st.ap(), in_=_sb[:]),
                 reads=[_sb], dma=True, tag="dbg_lnstats")
        if "xn" in debug:
            o_ = dbg_out("xn", [NK * 128, T], BF16)
            for k in range(NK):
                P.op("gpsimd", lambda e, k=k, o_=o_: e.dma_start(out=o_[k*128:(k+1)*128, :], in_=xn[k][:]),
                     reads=[xn[k]], dma=True, tag=f"dbg_xn{k}")

    if "s1only" in debug:
        return nc, P, st_global, dbg
    P.barrier()

    gather_units = []
    coll_marks = []

    def emit_gather_plane(which, src, mine, kk, bufs):
        (gi_sb3, gc_sb3, cbfT, gacc, gtmp2, _gt) = bufs
        N = NI

        NQ = (N + 3) // 4

        def nt(q):
            return min(4, N - q * 4)

        def ld(q):
            gi_sb = gi_sb3[q % 2]
            for t in range(nt(q)):
                i = q * 4 + t
                P.op("sync", lambda e, i=i, t=t, gi_sb=gi_sb: e.dma_start(
                        out=gi_sb[:, t*128:(t+1)*128],
                        in_=src[i*128:(i+1)*128, kk*128:(kk+1)*128]),
                     writes=[gi_sb], dma=True, tag=f"g{which}{kk}_ld{i}")

        def cast(q):
            gi_sb = gi_sb3[q % 2]; gc_sb = gc_sb3[q % 2]
            w = nt(q) * 128
            P.op("scalar", lambda e, gi_sb=gi_sb, gc_sb=gc_sb, w=w: e.copy(
                    gc_sb[:, 0:w], gi_sb[:, 0:w]),
                 reads=[gi_sb], writes=[gc_sb], tag=f"g{which}{kk}_cast{q}")

        def trev(q):
            gc_sb = gc_sb3[q % 2]
            bank4, base, bnm = txquad()
            for t in range(nt(q)):
                TR(bank4[:, (base+t)*128:(base+t+1)*128], bnm,
                   gc_sb[:, t*128:(t+1)*128], tag=f"g{which}{kk}_tr{q}_{t}")
            w = nt(q) * 128
            P.op("scalar", lambda e, q=q, bank4=bank4, base=base, w=w: e.copy(
                    cbfT[:, q*512:q*512+w], bank4[:, base*128:base*128+w]),
                 reads=[bnm], writes=[cbfT], tag=f"g{which}{kk}_ev{q}")

        def upro():
            ld(0); cast(0)
        gather_units.append(upro)
        for q in range(NQ):
            def ua(q=q):
                if q + 1 < NQ:
                    ld(q + 1)
                    cast(q + 1)
                trev(q)
            gather_units.append(ua)
        W = N * 128
        Wh = W - W // 2
        for hw in range(2):
            c0 = hw * Wh
            c1 = min(W, c0 + Wh)
            def u0(c0=c0, c1=c1):
                P.op("vector", lambda e, c0=c0, c1=c1: e.tensor_scalar(
                        gacc[:, c0:c1], cbfT[:, c0:c1], 0.0, lutb[:, 0:1],
                        op0=OP.is_equal, op1=OP.mult),
                     reads=[cbfT], sreads=[lutb], writes=[gacc], tag=f"g{which}{kk}_ts0_{hw}")
            gather_units.append(u0)
            for j in range(1, 16):
                def uj(j=j, c0=c0, c1=c1, hw=hw):
                    gt = gtmp2[j % 2]; gtn = f"gtmp{j % 2}"
                    P.op("vector", lambda e, j=j, c0=c0, c1=c1, gt=gt: e.tensor_scalar(
                            gt[:, 0:c1-c0], cbfT[:, c0:c1], float(j), lutb[:, j:j+1],
                            op0=OP.is_equal, op1=OP.mult),
                         reads=[cbfT], sreads=[lutb], writes=[gtn], tag=f"g{which}{kk}_ts{j}_{hw}")
                    P.op("vector", lambda e, j=j, c0=c0, c1=c1, gt=gt: e.tensor_add(
                            gacc[:, c0:c1], gacc[:, c0:c1], gt[:, 0:c1-c0]),
                         reads=[gacc, gtn], writes=[gacc], tag=f"g{which}{kk}_add{j}_{hw}")
                gather_units.append(uj)

        def ust():
            P.op("scalar", lambda e: e.dma_start(
                    out=mine[kk*128:(kk+1)*128, :], in_=gacc[:, 0:W]),
                 reads=[gacc], writes=[mine], dma=True, tag=f"g{which}{kk}_st")
        gather_units.append(ust)

    def emit_gather_dslab(s, bufs):
        (gi_sb3, gc_sb3, cbfT, gacc, gtmp2, _gt) = bufs
        N = NK

        NQ = N // 4

        def ld(q):
            gi_sb = gi_sb3[q % 2]
            for t in range(4):
                o2 = q * 4 + t
                P.op("sync", lambda e, o2=o2, t=t, s=s, gi_sb=gi_sb: e.dma_start(
                        out=gi_sb[:, t*128:(t+1)*128],
                        in_=didx_sl[o2*128:(o2+1)*128, s*128:(s+1)*128]),
                     writes=[gi_sb], dma=True, tag=f"gd{s}_ld{o2}")

        def cast(q):
            gi_sb = gi_sb3[q % 2]; gc_sb = gc_sb3[q % 2]
            P.op("scalar", lambda e, gi_sb=gi_sb, gc_sb=gc_sb: e.copy(gc_sb[:], gi_sb[:]),
                 reads=[gi_sb], writes=[gc_sb], tag=f"gd{s}_cast{q}")

        def trev(q):
            gc_sb = gc_sb3[q % 2]
            bank4, base, bnm = txquad()
            for t in range(4):
                TR(bank4[:, (base+t)*128:(base+t+1)*128], bnm,
                   gc_sb[:, t*128:(t+1)*128], tag=f"gd{s}_tr{q}_{t}")
            P.op("scalar", lambda e, q=q, bank4=bank4, base=base: e.copy(
                    cbfT[:, q*512:(q+1)*512, ], bank4[:, base*128:(base+4)*128]),
                 reads=[bnm], writes=[cbfT], tag=f"gd{s}_ev{q}")

        def upro():
            ld(0); cast(0)
        gather_units.append(upro)
        for q in range(NQ):
            def ua(q=q):
                if q + 1 < NQ:
                    ld(q + 1)
                    cast(q + 1)
                trev(q)
            gather_units.append(ua)
        for hw in range(2):
            c0 = hw * (D // 2); c1 = c0 + D // 2
            def u0(c0=c0, c1=c1):
                P.op("vector", lambda e, c0=c0, c1=c1: e.tensor_scalar(
                        gacc[:, c0:c1], cbfT[:, c0:c1], 0.0, lutb[:, 0:1],
                        op0=OP.is_equal, op1=OP.mult),
                     reads=[cbfT], sreads=[lutb], writes=[gacc], tag=f"gd{s}_ts0_{hw}")
            gather_units.append(u0)
            for j0 in range(1, 16, 2):
                def uj(j0=j0, c0=c0, c1=c1, hw=hw):
                    for j in range(j0, min(16, j0 + 2)):
                        gt = gtmp2[j % 2]; gtn = f"gtmp{j % 2}"
                        P.op("vector", lambda e, j=j, c0=c0, c1=c1, gt=gt: e.tensor_scalar(
                                gt[:, 0:D//2], cbfT[:, c0:c1], float(j), lutb[:, j:j+1],
                                op0=OP.is_equal, op1=OP.mult),
                             reads=[cbfT], sreads=[lutb], writes=[gtn], tag=f"gd{s}_ts{j}_{hw}")
                        P.op("vector", lambda e, j=j, c0=c0, c1=c1, gt=gt: e.tensor_add(
                                gacc[:, c0:c1], gacc[:, c0:c1], gt[:, 0:D//2]),
                             reads=[gacc, gtn], writes=[gacc], tag=f"gd{s}_add{j}_{hw}")
                gather_units.append(uj)

        def ust():
            for o2 in range(NK):
                P.op("scalar", lambda e, o2=o2, s=s: e.dma_start(
                        out=mine_d[o2*128:(o2+1)*128, s*128:(s+1)*128],
                        in_=gacc[:, o2*128:(o2+1)*128]),
                     reads=[gacc], writes=[mine_d], dma=True, tag=f"gd{s}_st{o2}")
        gather_units.append(ust)

    with contextlib.ExitStack() as s2:
        E2 = s2.enter_context
        gwide = max(NI * 128, D)
        TH_ = T // 4
        scr_a = E2(nc.sbuf_tensor("scr_a", [128, TH_], F32))
        scr_c = E2(nc.sbuf_tensor("scr_c", [128, TH_], F32))
        scr_i = E2(nc.sbuf_tensor("scr_i", [128, TH_], I32))
        gi_sb3 = [E2(nc.sbuf_tensor(f"gi_sb{j}", [128, 512], I32)) for j in range(2)]
        gc_sb3 = [E2(nc.sbuf_tensor(f"gc_sb{j}", [128, 512], BF16)) for j in range(2)]
        _gt = [0]
        cbfT = E2(nc.sbuf_tensor("cbfT", [128, gwide], BF16))
        gacc = E2(nc.sbuf_tensor("gacc", [128, gwide], BF16))
        gtmp2 = [E2(nc.sbuf_tensor("gtmp0", [128, gwide // 2], BF16))] * 2
        gbufs = (gi_sb3, gc_sb3, cbfT, gacc, gtmp2, _gt)
        emit_gather_plane("g", gidx_sl, mine_g, 0, gbufs)
        emit_gather_plane("g", gidx_sl, mine_g, 1, gbufs)
        coll_marks.append((len(gather_units), "cc_g", mine_g, allg_g))
        emit_gather_plane("u", uidx_sl, mine_u, 0, gbufs)
        emit_gather_plane("u", uidx_sl, mine_u, 1, gbufs)
        coll_marks.append((len(gather_units), "cc_u", mine_u, allg_u))
        for s in range(NDC):
            emit_gather_dslab(s, gbufs)
        coll_marks.append((len(gather_units), "cc_d", mine_d, allg_d))
        gb_i = 0

        def emit_gather_progress(n):
            nonlocal gb_i
            for _ in range(n):
                if gb_i >= len(gather_units):
                    return
                gather_units[gb_i]()
                gb_i += 1
                for (at, tag, mine, allg) in coll_marks:
                    if at == gb_i:
                        P.op("gpsimd", lambda e, mine=mine, allg=allg: e.collective_compute(
                                "AllGather", OP.bypass, replica_groups=[list(range(8))],
                                ins=[mine.ap().opt()], outs=[allg.ap().opt()]),
                             reads=[mine], writes=[allg], tag=tag, nobar=True)

        wq_sb = [E2(nc.sbuf_tensor(f"wq_sb{j}", [128, D], BF16)) for j in range(2)]
        wk_sb = [E2(nc.sbuf_tensor(f"wk_sb{j}", [128, D], BF16)) for j in range(2)]
        wv_sb = [E2(nc.sbuf_tensor(f"wv_sb{j}", [128, D], BF16)) for j in range(2)]
        q_fm = E2(nc.sbuf_tensor("q_fm", [128, T], BF16))
        k_fm = E2(nc.sbuf_tensor("k_fm", [128, T], BF16))
        v_fm = E2(nc.sbuf_tensor("v_fm", [128, T], BF16))
        cos_h = E2(nc.sbuf_tensor("cos_h", [128, T], BF16))
        sin_h = E2(nc.sbuf_tensor("sin_h", [128, T], BF16))
        qh2 = [E2(nc.sbuf_tensor(f"qh2_{j}", [128, T], BF16)) for j in range(2)]
        kh2 = [E2(nc.sbuf_tensor(f"kh2_{j}", [128, T], BF16)) for j in range(2)]
        ktm = E2(nc.sbuf_tensor("ktm", [128, NCHUNK * 256], BF16))
        vtm = E2(nc.sbuf_tensor("vtm", [128, NCHUNK * 128], BF16))
        sT4_sb = E2(nc.sbuf_tensor("sT4_sb", [128, 512], BF16))
        qt4_sb = E2(nc.sbuf_tensor("qt4_sb", [128, 1024], BF16))
        st_f = E2(nc.sbuf_tensor("st_f", [128, 256], F32))
        st_bf4 = E2(nc.sbuf_tensor("st_bf4", [128, 1024], BF16))
        dmask4 = E2(nc.sbuf_tensor("dmask4", [128, 512], BF16))
        qdec4 = E2(nc.sbuf_tensor("qdec4", [128, 512], BF16))
        y_stg = [E2(nc.sbuf_tensor("y_stg0", [128, T], BF16))] * 2

        NTC = T // 512 if T >= 512 else 1
        TCW = min(512, T)

        wrowf2 = [E2(nc.sbuf_tensor(f"wrowf{j}", [128, D // 2], F32)) for j in range(2)]
        ident_f = E2(nc.sbuf_tensor("ident_f", [128, 128], F32))
        P.op("vector", lambda e: e.tensor_copy(ident_f[:], ident[:]),
             reads=[ident], writes=[ident_f], tag="identf")
        _wt = [0]

        def emit_wT(orow_global, dstbuf, dst_hbm, tag):
            srcten = wqkv if orow_global < 3 * NK else wproj
            ro = orow_global if orow_global < 3 * NK else orow_global - 3 * NK
            buf = dstbuf if dstbuf is not None else kh2[1]
            for half in range(2):
                wrowf = wrowf2[_wt[0] % 2]
                _wt[0] += 1
                P.op("sync", lambda e, ro=ro, srcten=srcten, wrowf=wrowf, half=half:
                     e.dma_start(out=wrowf[:],
                                 in_=srcten[ro*128:(ro+1)*128, half*(D//2):(half+1)*(D//2)]),
                     writes=[wrowf], dma=True, tag=f"{tag}_ldf{half}")
                for kx in range(NK // 2):
                    k = half * (NK // 2) + kx
                    pfb = pq[0] if (k // 4) % 2 == 0 else pq[2]
                    bank = "pq0" if (k // 4) % 2 == 0 else "pq2"
                    pslot = pfb[:, (k % 4)*128:(k % 4 + 1)*128]
                    P.op("tensor", lambda e, pslot=pslot, wrowf=wrowf, kx=kx:
                         e.transpose(pslot, wrowf[:, kx*128:(kx+1)*128], ident_f[:]),
                         reads=[wrowf, ident_f], writes=[bank], tag=f"{tag}_tr{k}")
                    if k % 2 == 0:
                        P.op("scalar", lambda e, k=k, buf=buf, pslot=pslot:
                             e.copy(buf[:, k*128:(k+1)*128], pslot),
                             reads=[bank], writes=[buf], tag=f"{tag}_ev{k}")
                    else:
                        P.op("vector", lambda e, k=k, buf=buf, pslot=pslot:
                             e.tensor_copy(buf[:, k*128:(k+1)*128], pslot),
                             reads=[bank], writes=[buf], tag=f"{tag}_ev{k}")
            if dst_hbm is not None:
                P.op("sync", lambda e, dst_hbm=dst_hbm, buf=buf: e.dma_start(
                        out=dst_hbm, in_=buf[:]),
                     reads=[buf], writes=[wprojT_h], dma=True, tag=f"{tag}_st")

        def emit_sincos_unit(h, hf_, off, dst, part):
            if part == 0:
                P.op("vector", lambda e, h=h, hf_=hf_: e.tensor_scalar(
                        scr_a[:], iota_t[:, hf_*TH_:(hf_+1)*TH_], th2pi[:, h:h+1], None, op0=OP.mult),
                     reads=[iota_t], sreads=[th2pi], writes=[scr_a], tag=f"h{h}_u{hf_}{off}")
                if off:
                    P.op("vector", lambda e: e.tensor_scalar(
                            scr_a[:], scr_a[:], 0.25, None, op0=OP.add),
                         reads=[scr_a], writes=[scr_a], tag=f"h{h}_uo{hf_}")
                P.op("vector", lambda e: e.tensor_copy(scr_i[:], scr_a[:]),
                     reads=[scr_a], writes=[scr_i], tag=f"h{h}_i{hf_}{off}")
            elif part == 1:
                P.op("vector", lambda e: e.tensor_copy(scr_c[:], scr_i[:]),
                     reads=[scr_i], writes=[scr_c], tag=f"h{h}_f{hf_}{off}")
                P.op("vector", lambda e: e.tensor_sub(scr_c[:], scr_a[:], scr_c[:]),
                     reads=[scr_a, scr_c], writes=[scr_c], tag=f"h{h}_r{hf_}{off}")
            else:
                P.op("vector", lambda e: e.scalar_tensor_tensor(scr_c[:], scr_c[:], 0.5,
                        scr_c[:], op0=OP.is_gt, op1=OP.subtract),
                     reads=[scr_c], writes=[scr_c], tag=f"h{h}_w1{hf_}{off}")
                P.op("vector", lambda e: e.scalar_tensor_tensor(scr_c[:], scr_c[:], 0.5,
                        scr_c[:], op0=OP.is_gt, op1=OP.subtract),
                     reads=[scr_c], writes=[scr_c], tag=f"h{h}_w2{hf_}{off}")
                P.op("scalar", lambda e, dst=dst, hf_=hf_: e.activation(
                        dst[:, hf_*TH_:(hf_+1)*TH_], scr_c[:], AF.Sin, scale=2 * math.pi),
                     reads=[scr_c], writes=[dst], tag=f"h{h}_a{hf_}{off}")

        def emit_qkv_unit(h, wt, dstf, nmq, tc):
            pb = pq[0] if tc % 2 == 0 else pq[2]
            pbn = "pq0" if tc % 2 == 0 else "pq2"
            for k in range(NK):
                MM(pb[:, 0:TCW], pbn, wt[:, k*128:(k+1)*128],
                   xn[k][:, tc*TCW:(tc+1)*TCW], start=(k == 0), stop=(k == NK - 1),
                   tag=f"h{h}_{nmq}mm{k}_{tc}")
            P.op("scalar", lambda e, dstf=dstf, tc=tc, pb=pb: e.copy(
                    dstf[:, tc*TCW:(tc+1)*TCW], pb[:, 0:TCW]),
                 reads=[pbn], writes=[dstf], tag=f"h{h}_{nmq}ev{tc}")

        def A_units(h):
            wq, wk, wv = wq_sb[h % 2], wk_sb[h % 2], wv_sb[h % 2]
            units = []
            units.append(lambda: emit_wT(h, wq, None, f"h{h}_wq"))
            for tc in range(NTC):
                units.append(lambda tc=tc, wq=wq: emit_qkv_unit(h, wq, q_fm, "q", tc))
            units.append(lambda: emit_wT(NK + h, wk, None, f"h{h}_wk"))
            for tc in range(NTC):
                units.append(lambda tc=tc, wk=wk: emit_qkv_unit(h, wk, k_fm, "k", tc))
            units.append(lambda: emit_wT(2 * NK + h, wv, None, f"h{h}_wv"))
            for tc in range(NTC):
                units.append(lambda tc=tc, wv=wv: emit_qkv_unit(h, wv, v_fm, "v", tc))
            if h >= H - 8:
                for j_ in range(2):
                    o = (h - (H - 8)) * 2 + j_
                    units.append(lambda o=o: emit_wT(3 * NK + o, None,
                                                     wprojT_h[o*128:(o+1)*128, :], f"wp{o}"))
            for hf_ in range(4):
                for part in range(3):
                    units.append(lambda hf_=hf_, part=part: emit_sincos_unit(h, hf_, 0.0, sin_h, part))
                for part in range(3):
                    units.append(lambda hf_=hf_, part=part: emit_sincos_unit(h, hf_, 0.25, cos_h, part))
            return units

        def B_pre(h):
            s = SLOPES[h]
            P.op("scalar", lambda e, s=s: e.activation(
                    qdec4[:], iota_rep[:], AF.Exp, bias=-s, scale=-s),
                 reads=[iota_rep], writes=[qdec4], tag=f"h{h}_qdec")
            P.op("scalar", lambda e, s=s: e.activation(
                    dmask4[:], ddc_rep[:], AF.Exp, scale=-s),
                 reads=[ddc_rep], writes=[dmask4], tag=f"h{h}_em")
            P.op("vector", lambda e: e.tensor_mul(dmask4[:], dmask4[:], m01_rep[:]),
                 reads=[dmask4, m01_rep], writes=[dmask4], tag=f"h{h}_dm")
            P.op("vector", lambda e: e.tensor_mul(qh2[0][:], q_fm[:], cos_h[:]),
                 reads=[q_fm, cos_h], writes=[qh2[0]], tag=f"h{h}_lrq0")
            P.op("vector", lambda e: e.tensor_mul(qh2[1][:], q_fm[:], sin_h[:]),
                 reads=[q_fm, sin_h], writes=[qh2[1]], tag=f"h{h}_lrq1")
            P.op("vector", lambda e: e.tensor_mul(kh2[0][:], k_fm[:], cos_h[:]),
                 reads=[k_fm, cos_h], writes=[kh2[0]], tag=f"h{h}_lrk0")
            P.op("vector", lambda e: e.tensor_mul(kh2[1][:], k_fm[:], sin_h[:]),
                 reads=[k_fm, sin_h], writes=[kh2[1]], tag=f"h{h}_lrk1")
            for ip in range(NCHUNK // 2):
                bank4, base, bnm = txquad()
                for half4 in range(4):
                    i = ip * 2 + half4 // 2
                    half = half4 % 2
                    TR(bank4[:, (base+half4)*128:(base+half4+1)*128], bnm,
                       kh2[half][:, i*128:(i+1)*128], tag=f"h{h}_ktr{ip}_{half4}")
                P.op("scalar", lambda e, ip=ip, h=h, bank4=bank4, base=base: e.mul(
                        ktm[:, ip*512:(ip+1)*512], bank4[:, base*128:(base+4)*128],
                        kdec_all[:, h:h+1]),
                     reads=[bnm], sreads=[kdec_all], writes=[ktm], tag=f"h{h}_kev{ip}")
            for ip in range(NCHUNK // 4):
                bank4, base, bnm = txquad()
                for j4 in range(4):
                    i = ip * 4 + j4
                    TR(bank4[:, (base+j4)*128:(base+j4+1)*128], bnm,
                       v_fm[:, i*128:(i+1)*128], tag=f"h{h}_vtr{ip}_{j4}")
                P.op("scalar", lambda e, ip=ip, bank4=bank4, base=base: e.copy(
                        vtm[:, ip*512:(ip+1)*512], bank4[:, base*128:(base+4)*128]),
                     reads=[bnm], writes=[vtm], tag=f"h{h}_vev{ip}")

        def C1(h, sc):
            sccols = slice(sc * 512, (sc + 1) * 512)
            for j in range(4):
                i = sc * 4 + j
                cols = slice(i * 128, (i + 1) * 128)
                MM(pf[0][:, j*128:(j+1)*128], "pf0", kh2[0][:, cols], qh2[0][:, cols],
                   True, False, tag=f"h{h}_c{i}_s0")
                MM(pf[0][:, j*128:(j+1)*128], "pf0", kh2[1][:, cols], qh2[1][:, cols],
                   False, True, tag=f"h{h}_c{i}_s1")
            P.op("vector", lambda e: e.tensor_tensor(
                    sT4_sb[:], pf[0][:], dmask4[:], op=OP.mult),
                 reads=["pf0", dmask4], writes=[sT4_sb], tag=f"h{h}_sc{sc}_mask")
            P.op("vector", lambda e, sccols=sccols: e.tensor_tensor(
                    qt4_sb[:, 0:512], qh2[0][:, sccols], qdec4[:], op=OP.mult),
                 reads=[qh2[0], qdec4], writes=[qt4_sb], tag=f"h{h}_sc{sc}_qt0")
            P.op("vector", lambda e, sccols=sccols: e.tensor_tensor(
                    qt4_sb[:, 512:1024], qh2[1][:, sccols], qdec4[:], op=OP.mult),
                 reads=[qh2[1], qdec4], writes=[qt4_sb], tag=f"h{h}_sc{sc}_qt1")
            for j in range(4):
                i = sc * 4 + j
                if i >= NCHUNK - 1:
                    continue
                cols = slice(i * 128, (i + 1) * 128)
                pdl = pq[1] if j < 2 else pf[0]
                pdn = "pq1" if j < 2 else "pf0"
                co = (j % 2) * 256
                MM(pdl[:, co:co+128], pdn, ktm[:, i*256:i*256+128], vtm[:, cols],
                   True, True, tag=f"h{h}_c{i}_d0")
                MM(pdl[:, co+128:co+256], pdn, ktm[:, i*256+128:i*256+256], vtm[:, cols],
                   True, True, tag=f"h{h}_c{i}_d1")

        def C2(h, sc, ystg):
            po = pq[3] if sc % 2 == 0 else pf[1]
            po_n = "pq3" if sc % 2 == 0 else "pf1"
            def upd(j):
                i = sc * 4 + j
                if i >= NCHUNK - 1:
                    return
                pdl = pq[1] if j < 2 else pf[0]
                pdn = "pq1" if j < 2 else "pf0"
                co = (j % 2) * 256
                if i == 0:
                    P.op("vector", lambda e, pdl=pdl, co=co: e.tensor_copy(
                            st_f[:], pdl[:, co:co+256]),
                         reads=[pdn], writes=[st_f], tag=f"h{h}_c{i}_st01")
                else:
                    bd = BDECAY[h]
                    P.op("vector", lambda e, bd=bd, pdl=pdl, co=co: e.scalar_tensor_tensor(
                            st_f[:], st_f[:], bd, pdl[:, co:co+256], op0=OP.mult, op1=OP.add),
                         reads=[st_f, pdn], writes=[st_f], tag=f"h{h}_c{i}_sa")
                slot = ((i + 1) % 4) * 256
                P.op("vector", lambda e, slot=slot: e.tensor_copy(
                        st_bf4[:, slot:slot+256], st_f[:]),
                     reads=[st_f], writes=[st_bf4], tag=f"h{h}_c{i}_stbf")
            for j in range(3):
                upd(j)
            first = True
            for j in range(4):
                i = sc * 4 + j
                cols = slice(i * 128, (i + 1) * 128)
                last = (j == 3) and (sc * 4 == 0 and False)
                MM(po[:, j*128:(j+1)*128], po_n, vtm[:, cols], sT4_sb[:, j*128:(j+1)*128],
                   first, False, tag=f"h{h}_c{i}_o0")
                first = False
            nmm = []
            for j in range(4):
                i = sc * 4 + j
                if i == 0:
                    continue
                slot = (i % 4) * 256
                nmm.append((j, slot))
            for idx, (j, slot) in enumerate(nmm):
                i = sc * 4 + j
                stop = (idx == len(nmm) - 1)
                MM(po[:, j*128:(j+1)*128], po_n, st_bf4[:, slot:slot+128],
                   qt4_sb[:, j*128:(j+1)*128], False, False, tag=f"h{h}_c{i}_o1")
                MM(po[:, j*128:(j+1)*128], po_n, st_bf4[:, slot+128:slot+256],
                   qt4_sb[:, 512+j*128:512+(j+1)*128], False, stop, tag=f"h{h}_c{i}_o2")
            if not nmm:
                pass
            P.op("scalar", lambda e, sc=sc, ystg=ystg, po=po: e.copy(
                    ystg[:, sc*512:(sc+1)*512], po[:]),
                 reads=[po_n], writes=[ystg], tag=f"h{h}_sc{sc}_yev")
            upd(3)

        for h in range(H + 1):
            units = A_units(h) if h < H else []
            ai = 0
            if h >= 1:
                ystg = y_stg[(h - 1) % 2]
                for sc in range(NCHUNK // 4):
                    C1(h - 1, sc)
                    for _ in range(6):
                        if ai < len(units):
                            units[ai](); ai += 1
                    C2(h - 1, sc, ystg)
                    for _ in range(5):
                        if ai < len(units):
                            units[ai](); ai += 1
                    emit_gather_progress(5)
                P.op("scalar", lambda e, h=h, ystg=ystg: e.dma_start(
                        out=y_h[(h-1)*128:h*128, :], in_=ystg[:]),
                     reads=[ystg], writes=[y_h], dma=True, tag=f"h{h-1}_yst")
            while ai < len(units):
                units[ai](); ai += 1
            if h == 0:
                emit_gather_progress(16)
            if h < H:
                B_pre(h)
        emit_gather_progress(10**9)

        if "y" in debug:
            o_ = dbg_out("y", [NK * 128, T], BF16)
            P.op("gpsimd", lambda e, o_=o_: e.dma_start(out=o_.ap(), in_=y_h.ap()),
                 reads=["y_h"], dma=True, tag="dbg_y")
        if "gather" in debug:
            om = dbg_out("mine_d", [NK * 128, NDC * 128], BF16)
            P.op("gpsimd", lambda e, om=om: e.dma_start(out=om.ap(), in_=mine_d.ap()),
                 reads=[mine_d], dma=True, tag="dbg_md")
            o_ = dbg_out("allg_g", [8 * 2 * 128, NI * 128], BF16)
            P.op("gpsimd", lambda e, o_=o_: e.dma_start(out=o_.ap(), in_=allg_g.ap()),
                 reads=[allg_g], dma=True, tag="dbg_gg")
            o2_ = dbg_out("allg_d", [8 * NK * 128, NDC * 128], BF16)
            P.op("gpsimd", lambda e, o2_=o2_: e.dma_start(out=o2_.ap(), in_=allg_d.ap()),
                 reads=[allg_d], dma=True, tag="dbg_gd")

    P.barrier()

    st_tab.close()
    with contextlib.ExitStack() as s3:
        E3 = s3.enter_context
        y_sb = [E3(nc.sbuf_tensor(f"y_sb{k}", [128, TCW], BF16)) for k in range(NK)]
        wpass = [E3(nc.sbuf_tensor(f"wpass{j}", [128, D], BF16)) for j in range(8)]
        s_sb = [E3(nc.sbuf_tensor(f"s_sb{j}", [128, TCW], BF16)) for j in range(4)]
        for tc in range(NTC):
            for k in range(NK):
                P.op("sync", lambda e, k=k, tc=tc: e.dma_start(
                        out=y_sb[k][:], in_=y_h[k*128:(k+1)*128, tc*TCW:(tc+1)*TCW]),
                     writes=[y_sb[k]], dma=True, tag=f"p{tc}_yld{k}")
            for oq in range(4):
                for o4 in range(4):
                    o = oq * 4 + o4
                    wp = wpass[(oq % 2) * 4 + o4]
                    P.op("sync", lambda e, o=o, wp=wp: e.dma_start(
                            out=wp[:], in_=wprojT_h[o*128:(o+1)*128, :]),
                         writes=[wp], dma=True, tag=f"p{tc}_{oq}_wld{o4}")
                    for k in range(NK):
                        MM(pq[o4][:, 0:TCW], f"pq{o4}", wp[:, k*128:(k+1)*128], y_sb[k][:],
                           start=(k == 0), stop=(k == NK - 1), tag=f"p{tc}_{oq}_mm{o4}_{k}")
                    P.op("vector", lambda e, o4=o4: e.tensor_copy(s_sb[o4][:], pq[o4][:, 0:TCW]),
                         reads=[f"pq{o4}"], writes=[s_sb[o4]], tag=f"p{tc}_{oq}_ev{o4}")
                    P.op("scalar", lambda e, o=o, tc=tc, o4=o4: e.dma_start(
                            out=s_h[o*128:(o+1)*128, tc*TCW:(tc+1)*TCW], in_=s_sb[o4][:]),
                         reads=[s_sb[o4]], writes=[s_h], dma=True, tag=f"p{tc}_{oq}_st{o4}")
        if "s" in debug:
            o_ = dbg_out("s", [NK * 128, T], BF16)
            P.op("gpsimd", lambda e, o_=o_: e.dma_start(out=o_.ap(), in_=s_h.ap()),
                 reads=["s_h"], dma=True, tag="dbg_s")

    P.barrier()

    st_p123.close()
    NQUAD = (NI + 3) // 4
    NTC4 = T // 512
    with contextlib.ExitStack() as s4a:
        E4 = s4a.enter_context
        pg = [E4(nc.psum_tensor(f"pg{t}", [128, 512], F32)) for t in range(NTC4)]
        pu = [E4(nc.psum_tensor(f"pu{t}", [128, 512], F32)) for t in range(NTC4)]
        wg_win = [E4(nc.sbuf_tensor(f"wg_win{j}", [128, NK * 512], BF16)) for j in range(2)]
        wu_win = [E4(nc.sbuf_tensor(f"wu_win{j}", [128, NK * 512], BF16)) for j in range(2)]
        gsil = E4(nc.sbuf_tensor("gsil", [128, T], BF16))
        h_t = [E4(nc.sbuf_tensor(f"h_t{j}", [128, T], BF16)) for j in range(2)]
        for iq in range(NQUAD):
            qn = min(4, NI - iq * 4)
            wgw = wg_win[iq % 2]; wuw = wu_win[iq % 2]
            wc0, wc1 = iq * 512, iq * 512 + qn * 128
            for k in range(NK):
                c, kk = k // 2, k % 2
                row = (c * 2 + kk) * 128
                P.op("sync", lambda e, k=k, row=row, wgw=wgw, wc0=wc0, wc1=wc1, qn=qn:
                     e.dma_start(out=wgw[:, k*512:k*512+qn*128], in_=allg_g[row:row+128, wc0:wc1]),
                     reads=[allg_g], writes=[wgw], dma=True, tag=f"fg_w{iq}_{k}")
                P.op("scalar", lambda e, k=k, row=row, wuw=wuw, wc0=wc0, wc1=wc1, qn=qn:
                     e.dma_start(out=wuw[:, k*512:k*512+qn*128], in_=allg_u[row:row+128, wc0:wc1]),
                     reads=[allg_u], writes=[wuw], dma=True, tag=f"fu_w{iq}_{k}")
            for q in range(qn):
                i = iq * 4 + q
                ht = h_t[i % 2]
                for k in range(NK):
                    wsl = wgw[:, k*512+q*128:k*512+(q+1)*128]
                    for tc in range(NTC4):
                        MM(pg[tc][:], f"pg{tc}", wsl, xn[k][:, tc*512:(tc+1)*512],
                           start=(k == 0), stop=(k == NK - 1), extra_reads=[xn[k]],
                           tag=f"fg{i}_mm{k}_{tc}")
                for tc in range(NTC4):
                    P.op("scalar", lambda e, tc=tc: e.activation(
                            gsil[:, tc*512:(tc+1)*512], pg[tc][:], AF.Silu),
                         reads=[f"pg{tc}"], writes=[gsil], tag=f"fg{i}_sil{tc}")
                for k in range(NK):
                    wsl = wuw[:, k*512+q*128:k*512+(q+1)*128]
                    for tc in range(NTC4):
                        MM(pu[tc][:], f"pu{tc}", wsl, xn[k][:, tc*512:(tc+1)*512],
                           start=(k == 0), stop=(k == NK - 1), extra_reads=[xn[k]],
                           tag=f"fu{i}_mm{k}_{tc}")
                for tc in range(NTC4):
                    P.op("vector", lambda e, tc=tc, ht=ht: e.tensor_tensor(
                            ht[:, tc*512:(tc+1)*512], gsil[:, tc*512:(tc+1)*512],
                            pu[tc][:], op=OP.mult),
                         reads=[gsil, f"pu{tc}"], writes=[ht], tag=f"fh{i}_{tc}")
                P.op("sync", lambda e, i=i, ht=ht: e.dma_start(
                        out=h_hbm[i*128:(i+1)*128, :], in_=ht[:]),
                     reads=[ht], writes=[h_hbm], dma=True, tag=f"fh{i}_st")

    if "h" in debug:
        oh = dbg_out("h", [NI * 128, T], BF16)
        P.op("gpsimd", lambda e, oh=oh: e.dma_start(out=oh.ap(), in_=h_hbm.ap()),
             reads=[h_hbm], dma=True, tag="dbg_h")
    st_xn.close()
    P.barrier()

    with contextlib.ExitStack() as s4b:
        E4 = s4b.enter_context
        pd = [E4(nc.psum_tensor(f"pd{j}", [128, 512], F32)) for j in range(4)]
        ptr3 = [E4(nc.psum_tensor("ptr3", [128, 1024], BF16)),
                E4(nc.psum_tensor("ptr4", [128, 1024], BF16))]
        _tx4 = [0]
        def txslot4():
            i = _tx4[0] % 16
            _tx4[0] += 1
            return ptr3[i // 8][:, (i % 8)*128:(i % 8)*128+128], f"ptr34{i // 8}"
        hslab = E4(nc.sbuf_tensor("hslab", [128, NI * 1024], BF16))
        wd_sb = [E4(nc.sbuf_tensor(f"wd_sb{j}", [128, 8 * NDC * 128], BF16)) for j in range(2)]
        g_sb = E4(nc.sbuf_tensor("g_sb", [128, 1024], BF16))
        s_t = [E4(nc.sbuf_tensor(f"s_t{j}", [128, 1024], BF16)) for j in range(2)]
        sg_sb = E4(nc.sbuf_tensor("sg_sb", [128, 1024], BF16))
        sg_tm = [E4(nc.sbuf_tensor(f"sg_tm{b}", [128, D], BF16)) for b in range(8)]
        xrow = [E4(nc.sbuf_tensor(f"xrow{j}", [128, D], F32)) for j in range(2)]
        orow = [E4(nc.sbuf_tensor(f"orow{j}", [128, D], F32)) for j in range(2)]
        for tc in range(2):
            tcsl = slice(tc * 1024, (tc + 1) * 1024)
            for i in range(NI):
                P.op("sync" if i % 2 == 0 else "scalar",
                     lambda e, i=i, tcsl=tcsl: e.dma_start(
                        out=hslab[:, i*1024:(i+1)*1024], in_=h_hbm[i*128:(i+1)*128, tcsl]),
                     reads=[h_hbm], writes=[f"hslab{i}"], dma=True, tag=f"fd{tc}_h{i}")
            for o2 in range(NK):
                wd = wd_sb[o2 % 2]
                pda = pd[(o2 % 2) * 2]; pdb = pd[(o2 % 2) * 2 + 1]
                pda_n = f"pd{(o2 % 2) * 2}"; pdb_n = f"pd{(o2 % 2) * 2 + 1}"
                for c in range(8):
                    P.op("sync", lambda e, c=c, o2=o2, wd=wd: e.dma_start(
                            out=wd[:, c*NDC*128:(c+1)*NDC*128],
                            in_=allg_d[(c*NK + o2)*128:(c*NK + o2 + 1)*128, :]),
                         reads=[allg_d], writes=[wd], dma=True, tag=f"fd{tc}_{o2}_w{c}")
                for ci in range(NI):
                    c = ci % 8; s = ci // 8
                    wcol = (c * NDC + s) * 128
                    MM(pda[:], pda_n, wd[:, wcol:wcol+128], hslab[:, ci*1024:ci*1024+512],
                       start=(ci == 0), stop=(ci == NI - 1), extra_reads=[f"hslab{ci}"],
                       tag=f"fd{tc}_{o2}_mmA{ci}")
                    MM(pdb[:], pdb_n, wd[:, wcol:wcol+128], hslab[:, ci*1024+512:(ci+1)*1024],
                       start=(ci == 0), stop=(ci == NI - 1), extra_reads=[f"hslab{ci}"],
                       tag=f"fd{tc}_{o2}_mmB{ci}")
                st_ = s_t[o2 % 2]
                P.op("scalar", lambda e, o2=o2, tcsl=tcsl, st_=st_: e.dma_start(
                        out=st_[:], in_=s_h[o2*128:(o2+1)*128, tcsl]),
                     reads=[s_h], writes=[st_], dma=True, tag=f"fd{tc}_{o2}_sld")
                P.op("scalar", lambda e, pda=pda: e.activation(g_sb[:, 0:512], pda[:], AF.Silu),
                     reads=[pda_n], writes=[g_sb], tag=f"fd{tc}_{o2}_silA")
                P.op("scalar", lambda e, pdb=pdb: e.activation(g_sb[:, 512:1024], pdb[:], AF.Silu),
                     reads=[pdb_n], writes=[g_sb], tag=f"fd{tc}_{o2}_silB")
                P.op("vector", lambda e, st_=st_: e.tensor_mul(sg_sb[:], g_sb[:], st_[:]),
                     reads=[g_sb, st_], writes=[sg_sb], tag=f"fd{tc}_{o2}_sg")
                for b in range(8):
                    pslot, bank = txslot4()
                    TR(pslot, bank, sg_sb[:, b*128:(b+1)*128], tag=f"fd{tc}_{o2}_tr{b}")
                    P.op("vector", lambda e, b=b, o2=o2, pslot=pslot: e.tensor_copy(
                            sg_tm[b][:, o2*128:(o2+1)*128], pslot),
                         reads=[bank], writes=[sg_tm[b]], tag=f"fd{tc}_{o2}_tev{b}")
            for b in range(8):
                r = tc * 8 + b
                xr = xrow[b % 2]; orr = orow[b % 2]
                P.op("sync", lambda e, r=r, xr=xr: e.dma_start(
                        out=xr[:], in_=x_in[r*128:(r+1)*128, :]),
                     writes=[xr], dma=True, tag=f"o{r}_x")
                P.op("vector", lambda e, xr=xr, orr=orr, b=b: e.tensor_tensor(
                        orr[:], xr[:], sg_tm[b][:], op=OP.add),
                     reads=[xr, sg_tm[b]], writes=[orr], tag=f"o{r}_add")
                P.op("scalar", lambda e, r=r, orr=orr: e.dma_start(
                        out=out[r*128:(r+1)*128, :], in_=orr[:]),
                     reads=[orr], dma=True, tag=f"o{r}_st")

    return nc, P, st_global, dbg


def finalize(nc, P):
    with nc.Block() as block:
        stats = P.link_and_emit(nc, block)
    return nc, stats



def shard_inputs(inputs):
    x = np.ascontiguousarray(inputs["L0_attn_x"])
    gidx = inputs["L0_ffn_gate_idx"]; uidx = inputs["L0_ffn_up_idx"]
    didx = inputs["L0_ffn_down_idx"]
    maps = []
    for c in range(8):
        cis = [ci for ci in range(c, NI, 8)]
        while len(cis) < NDC:
            cis.append(NI - 1)
        dsl = np.concatenate([didx[:, ci*128:(ci+1)*128] for ci in cis], axis=1)
        maps.append({
            "x": x[c],
            "wqkv": np.ascontiguousarray(inputs["L0_attn_w_qkv"]),
            "wproj": np.ascontiguousarray(inputs["L0_attn_w_proj"]),
            "theta": np.ascontiguousarray(inputs["L0_attn_theta"]),
            "lut": np.ascontiguousarray(inputs["L0_ffn_lut"]),
            "gidx_sl": np.ascontiguousarray(gidx[:, DSL*c:DSL*(c+1)]),
            "uidx_sl": np.ascontiguousarray(uidx[:, DSL*c:DSL*(c+1)]),
            "didx_sl": np.ascontiguousarray(dsl),
        })
    return maps


_CACHE = {}

def kernel(L0_attn_x, L0_attn_w_qkv, L0_attn_w_proj, L0_attn_theta,
           L0_ffn_lut, L0_ffn_gate_idx, L0_ffn_up_idx, L0_ffn_down_idx):
    inputs = dict(L0_attn_x=np.asarray(L0_attn_x),
                  L0_attn_w_qkv=np.asarray(L0_attn_w_qkv),
                  L0_attn_w_proj=np.asarray(L0_attn_w_proj),
                  L0_attn_theta=np.asarray(L0_attn_theta),
                  L0_ffn_lut=np.asarray(L0_ffn_lut),
                  L0_ffn_gate_idx=np.asarray(L0_ffn_gate_idx),
                  L0_ffn_up_idx=np.asarray(L0_ffn_up_idx),
                  L0_ffn_down_idx=np.asarray(L0_ffn_down_idx))
    if "nc" not in _CACHE:
        nc, P, st, dbg = build_program()
        nc, stats = finalize(nc, P)
        _CACHE["nc"] = nc
    nc = _CACHE["nc"]
    maps = shard_inputs(inputs)
    res = run_bass_kernel_spmd(nc, maps, core_ids=list(range(8)))
    return np.stack([res.results[c]["out"] for c in range(8)], axis=0)


